# version 31
# baseline (speedup 1.0000x reference)
import contextlib
import numpy as np
import concourse.bass as bass
import concourse.mybir as mybir
from concourse.bass_utils import run_bass_kernel_spmd

F32 = mybir.dt.float32
BF16 = mybir.dt.bfloat16
U32 = mybir.dt.uint32
AF = mybir.ActivationFunctionType
OP = mybir.AluOpType
AX = mybir.AxisListType

ENGINES = ("sync", "gpsimd", "scalar", "vector", "tensor")
D = 1024
DC = 8
NSEG = 8
DH = 64


def f32(x):
    return np.float32(x)


def build_core_program(S=2048, H=16, debug=False):
    TC = S // 128
    NB = S // 512
    QHW = min(S, 1024)
    NQH = S // QHW
    NQB = QHW // 512
    NPAIR = H // 2
    HC = DH * H // 128
    FF = 2 * DH * H
    VHH = min(8, H)
    NV = H // VHH
    VW = (VHH // 2) * 161

    nc = bass.Bass("TRN2", target_bir_lowering=False)

    xT = nc.dram_tensor("xT", [D, S], F32, kind="ExternalInput").ap()
    wqk_lt = nc.dram_tensor("wqk_lt", [D, FF], F32, kind="ExternalInput").ap()
    wqk_st = nc.dram_tensor("wqk_st", [D, FF], F32, kind="ExternalInput").ap()
    wv_lt = nc.dram_tensor("wv_lt", [D, DH * H], F32, kind="ExternalInput").ap()
    wv_st = nc.dram_tensor("wv_st", [D, DH * H], F32, kind="ExternalInput").ap()
    wout_lt = nc.dram_tensor("wout_lt", [DH * H, D], F32, kind="ExternalInput").ap()
    wout_st = nc.dram_tensor("wout_st", [DH * H, D], F32, kind="ExternalInput").ap()
    fuswT = nc.dram_tensor("fuswT", [2 * D, D], F32, kind="ExternalInput").ap()
    w1d = nc.dram_tensor("w1", [D, D], F32, kind="ExternalInput").ap()
    w2d = nc.dram_tensor("w2", [D, D], F32, kind="ExternalInput").ap()
    w3d = nc.dram_tensor("w3", [D, D], F32, kind="ExternalInput").ap()
    protosT = nc.dram_tensor("protosT", [128, DC * NSEG], F32, kind="ExternalInput").ap()
    biases = nc.dram_tensor("biases", [128, 80], F32, kind="ExternalInput").ap()
    x1sc = nc.dram_tensor("x1sc", [D, S], F32, kind="Internal").ap()
    outT = nc.dram_tensor("outT", [D, S], F32, kind="ExternalOutput").ap()
    dbg = {}
    if debug:
        for nm, shp in [("d_sum", [128, 1024]), ("d_rep", [128, 512]),
                        ("d_ln1", [D, S]), ("d_seg", [S]), ("d_is1", [S]),
                        ("d_long", [D, S]), ("d_sel", [D, S]),
                        ("d_qg", [128, S]), ("d_kg", [128, S]),
                        ("d_ba", [D, S]), ("d_bs", [D, S]),
                        ("d_mask", [128, 1024]), ("d_me", [128, 1024]),
                        ("d_e0", [128, 1024]),
                        ("d_vq", [128, S]), ("d_vk", [128, S]),
                        ("d_ln2", [D, S]), ("d_x1", [D, S])]:
            dbg[nm] = nc.dram_tensor(nm, shp, F32, kind="ExternalOutput").ap()

    DSL = [f"dS{i}" for i in range(6)]
    DGL = [f"dG{i}" for i in range(4)]
    cnt = {s: 0 for s in DSL + DGL + ["sT", "sV", "sS", "sG"]}
    rrix = {"dS": 0, "dG": 0}
    hiw = {(e, s): 0 for e in ENGINES for s in cnt}
    Q = {e: [] for e in ENGINES}

    ctx = contextlib.ExitStack()
    with ctx as stack:
        E = stack.enter_context
        SEM = {s: E(nc.semaphore(s)) for s in cnt}

        A = E(nc.sbuf_tensor("A", [128, DC * S], BF16))
        Bb = E(nc.sbuf_tensor("Bb", [128, DC * S], BF16))
        Cb = E(nc.sbuf_tensor("Cb", [128, DC * S], BF16))

        qg = E(nc.sbuf_tensor("qg", [128, S], BF16))
        kg = E(nc.sbuf_tensor("kg", [128, S], BF16))
        vg = E(nc.sbuf_tensor("vg", [128, TC * VW], BF16))
        e0 = E(nc.sbuf_tensor("e0", [128, 2048], BF16))
        me0 = E(nc.sbuf_tensor("me0", [128, 2048], BF16))
        maskt = E(nc.sbuf_tensor("maskt", [128, 2 * 512], BF16))
        segrep = E(nc.sbuf_tensor("segrep", [128, S], BF16))
        outraw = E(nc.sbuf_tensor("outraw", [128, 1024], BF16))
        is1B = E(nc.sbuf_tensor("is1B", [128, S], mybir.dt.uint8))
        wst = E(nc.sbuf_tensor("wst", [128, 2048], F32))
        wbf = E(nc.sbuf_tensor("wbf", [128, 4096], BF16))
        xst = E(nc.sbuf_tensor("xst", [128, 4 * 512], F32))
        xbig = E(nc.sbuf_tensor("xbig", [128, 2 * S], F32))

        smallf = E(nc.sbuf_tensor("smallf", [128, 64], F32))
        smallb = E(nc.sbuf_tensor("smallb", [128, 3 * S + 64], BF16))
        identf = E(nc.sbuf_tensor("identf", [128, 128], F32))
        iotaF8 = E(nc.sbuf_tensor("iotaF8", [128, 8], F32))
        iotaP = E(nc.sbuf_tensor("iotaP", [128, 1], F32))
        iotaF128 = E(nc.sbuf_tensor("iotaF128", [128, 128], F32))
        ones_bf = E(nc.sbuf_tensor("ones_bf", [128, 128], BF16))
        ones_sc = E(nc.sbuf_tensor("ones_sc", [128, 128], BF16))
        onesf = E(nc.sbuf_tensor("onesf", [128, 128], F32))
        bias_sb = E(nc.sbuf_tensor("bias_sb", [128, 80], F32))
        bvb = E(nc.sbuf_tensor("bvb", [128, 16], BF16))
        protf = E(nc.sbuf_tensor("protf", [128, DC * 8], F32))
        prot_hi = E(nc.sbuf_tensor("prot_hi", [128, DC * 8], BF16))
        prot_lo = E(nc.sbuf_tensor("prot_lo", [128, DC * 8], BF16))
        ohN = E(nc.sbuf_tensor("ohN", [128, 8], F32))
        segu = E(nc.sbuf_tensor("segu", [128, 8], U32))
        segidN = E(nc.sbuf_tensor("segidN", [128, TC], F32))
        mx8 = E(nc.sbuf_tensor("mx8", [128, 1], F32))
        cntf = E(nc.sbuf_tensor("cntf", [8, 1], F32))
        cnt_rep = E(nc.sbuf_tensor("cnt_rep", [8, 128], BF16))
        iota8c = E(nc.sbuf_tensor("iota8c", [8, 1], BF16))

        wob = E(nc.sbuf_tensor("wob", [128, 16], F32))


        ps = [E(nc.psum_tensor(f"ps{i}", [128, 1024], F32)) for i in range(4)]
        rstdB = xbig[:, S:2 * S]
        recb2 = xst

        tanhs = e0
        nDst = S // 512
        DstX = E(nc.sbuf_tensor("DstX", [128, max(0, (DC - 2 * nDst)) * 512],
                                BF16)) if DC > 2 * nDst else None

        def Dst_ap(mb):
            if mb < nDst:
                return qg[:, (mb % nDst) * 512:(mb % nDst) * 512 + 512]
            if mb < 2 * nDst:
                m2 = mb - nDst
                return kg[:, m2 * 512:m2 * 512 + 512]
            m3 = mb - 2 * nDst
            return DstX[:, m3 * 512:m3 * 512 + 512]

        def emit(eng, fn):
            Q[eng].append(fn)

        def wait(eng, *deps):
            todo = []
            for d in deps:
                if d is None:
                    continue
                s, v = d
                if v is None:
                    continue
                if v is None or v <= 0:
                    continue
                if hiw[(eng, s)] >= v:
                    continue
                hiw[(eng, s)] = v
                todo.append((s, v))
            if todo:
                def f(h, todo=tuple(todo)):
                    for s, v in todo:
                        h.wait_ge(SEM[s], v)
                emit(eng, f)

        def run(eng, make, inc=None, k=1):
            if inc is None:
                emit(eng, lambda h, m=make: m(h))
                return None
            if inc in ("dS", "dG"):
                lanes = DSL if inc == "dS" else DGL
                s = lanes[rrix[inc] % len(lanes)]
                rrix[inc] += 1
                wait(eng, (s, cnt[s]))
                cnt[s] += 16
                v = cnt[s]

                def fd(h, m=make, s=s):
                    m(h).then_inc(SEM[s], 16)
                emit(eng, fd)
                return (s, v)
            cnt[inc] += k
            v = cnt[inc]

            def f(h, m=make, s=inc, kk=k):
                m(h).then_inc(SEM[s], kk)
            emit(eng, f)
            return v

        def all_lanes(pool):
            lanes = DSL if pool == "dS" else DGL
            return [(s, cnt[s]) for s in lanes]

        def til(t, m, c0=0, c1=None):
            c1 = S if c1 is None else c1
            return t[:, m * S + c0: m * S + c1]

        def psM(qb):
            return ps[qb // 2][:, (qb % 2) * 512:(qb % 2) * 512 + 512]

        def psq23(qb):
            return ps[2 + qb // 2][:, (qb % 2) * 512:(qb % 2) * 512 + 512]

        wstate = {"wst": [None, None, None, None],
                  "wkey": [0] * 8}

        def mark_wbf(lo, hi, val):
            for kk in range(lo // 512, (hi + 511) // 512):
                if wstate["wkey"][kk] < val:
                    wstate["wkey"][kk] = val

        def load_cols(dram, CT, col0, Wd, wst_slot, wbf_off, nchunk_lo=0,
                      nchunk_hi=DC, eng="gpsimd"):
            nch = nchunk_hi - nchunk_lo
            assert nch * Wd <= 512
            so = wst_slot * 512
            ws = wstate["wst"][wst_slot]
            wait("sync", ws if isinstance(ws, tuple) else None)
            dv = run("sync", lambda h: h.dma_start(
                bass.AP(wst[:].tensor, so, [wst[:].ap[0], [Wd, nch], [1, Wd]]),
                bass.AP(dram.tensor, nchunk_lo * 128 * CT + col0,
                        [[CT, 128], [128 * CT, nch], [1, Wd]])), "dS")
            deps = [dv] + [("sT", wstate["wkey"][kk])
                           for kk in range(wbf_off // 512,
                                           (wbf_off + nch * Wd + 511) // 512)]
            wait(eng, *deps)
            if eng == "scalar":
                v = run(eng, lambda h: h.activation(
                    wbf[:, wbf_off:wbf_off + nch * Wd],
                    wst[:, so:so + nch * Wd], AF.Copy), "sS")
                wstate["wst"][wst_slot] = ("sS", v)
                return ("sS", v)
            v = run(eng, lambda h: h.tensor_copy(
                wbf[:, wbf_off:wbf_off + nch * Wd],
                wst[:, so:so + nch * Wd]), "sG")
            wstate["wst"][wst_slot] = ("sG", v)
            return ("sG", v)

        for mk in (lambda h: h.iota(iotaF8[:], [[1, 8]], channel_multiplier=0,
                                    allow_small_or_imprecise_dtypes=True),
                   lambda h: h.iota(iotaP[:], [[1, 1]], channel_multiplier=1,
                                    allow_small_or_imprecise_dtypes=True),
                   lambda h: h.iota(iotaF128[:], [[1, 128]], channel_multiplier=0,
                                    allow_small_or_imprecise_dtypes=True),
                   lambda h: h.memset(ones_bf[:], 1.0),
                   lambda h: h.memset(ones_sc[:], 1.0 / 1024.0),
                   lambda h: h.memset(onesf[:], 1.0)):
            run("gpsimd", mk, "sG")
        run("gpsimd", lambda h: h.memset(
            bass.AP(vg[:].tensor, 64,
                    [vg[:].ap[0], [VW, TC], [161, VHH // 2], [1, 2]]), 1.0), "sG")
        run("gpsimd", lambda h: h.memset(
            bass.AP(vg[:].tensor, 66,
                    [vg[:].ap[0], [VW, TC], [161, VHH // 2], [1, 31]]), 0.0), "sG")
        sG0 = cnt["sG"]

        d_bias = run("sync", lambda h: h.dma_start(
            bias_sb[:, 0:80], biases[:, :]), "dS")
        d_prot = run("sync", lambda h: h.dma_start(
            protf[:, 0:DC * 8], protosT[:, :]), "dS")

        wait("vector", ("sG", sG0), d_prot, d_bias)
        run("vector", lambda h: h.tensor_scalar(
            identf[:], iotaF128[:], iotaP[:, 0:1], None, OP.is_equal), "sV")
        run("vector", lambda h: h.tensor_copy(iota8c[:], iotaP[0:8, 0:1]), "sV")
        run("vector", lambda h: h.tensor_copy(bvb[:, 0:8], bias_sb[:, 16:24]), "sV")
        run("vector", lambda h: h.tensor_copy(bvb[:, 8:16], bias_sb[:, 48:56]), "sV")
        wait("scalar", d_prot)
        s_ph = run("scalar", lambda h: h.activation(prot_hi[:], protf[:], AF.Copy), "sS")
        wait("vector", ("sS", s_ph))
        v_pl = run("vector", lambda h: h.tensor_tensor(
            prot_lo[:], protf[:], prot_hi[:], OP.subtract), "sV")
        v_const = cnt["sV"]

        casts, xlos = [], []
        for m in range(DC):
            sl = m % 2
            if m >= 2:
                wait("sync", ("sS", casts[m - 2]), ("sV", xlos[m - 2]))
            d = run("sync", lambda h, m=m, sl=sl: h.dma_start(
                xbig[:, sl * S:(sl + 1) * S], xT[m * 128:(m + 1) * 128, :]), "dS")
            wait("scalar", d)
            c = run("scalar", lambda h, m=m, sl=sl: h.activation(
                til(A, m), xbig[:, sl * S:(sl + 1) * S], AF.Copy), "sS")
            casts.append(c)
            wait("vector", d, ("sS", c))
            v = run("vector", lambda h, m=m, sl=sl: h.tensor_tensor(
                til(Bb, m), xbig[:, sl * S:(sl + 1) * S], til(A, m), OP.subtract), "sV")
            xlos.append(v)

        wait("tensor", ("sG", sG0))
        for qb in range(NB):
            for m in range(DC):
                wait("tensor", ("sS", casts[m]))
                run("tensor", lambda h, qb=qb, m=m: h.matmul(
                    psM(qb), ones_sc[:], til(A, m, qb * 512, qb * 512 + 512),
                    start=(m == 0), stop=False if m < DC - 1 else True),
                    "sT" if m == DC - 1 else None)
        mm_mean = cnt["sT"]

        wait("scalar", ("sT", mm_mean))
        for qb in range(NB):
            run("scalar", lambda h, qb=qb: h.activation(
                smallb[0:1, S + qb * 512:S + qb * 512 + 512], psM(qb)[0:1, :],
                AF.Copy), "sS")
        s_mh = cnt["sS"]
        wait("vector", ("sS", s_mh))
        for qb in range(NB):
            run("vector", lambda h, qb=qb: h.tensor_tensor(
                smallb[0:1, 2 * S + qb * 512:2 * S + qb * 512 + 512],
                psM(qb)[0:1, :],
                smallb[0:1, S + qb * 512:S + qb * 512 + 512], OP.subtract), "sV")
        v_ml = cnt["sV"]

        sprop = ps[2][0:1, 512:520]
        wait("tensor", ("sV", v_pl))
        for i, pr in enumerate([prot_hi, prot_lo]):
            for m in range(DC):
                last = (i == 1 and m == DC - 1)
                run("tensor", lambda h, pr=pr, m=m, i=i, last=last: h.matmul(
                    sprop, ones_bf[:, 0:1], pr[:, m * 8:(m + 1) * 8],
                    start=(i == 0 and m == 0), stop=last),
                    "sT" if last else None)
        mm_sp = cnt["sT"]
        wait("vector", ("sT", mm_sp))
        run("vector", lambda h: h.tensor_scalar(
            smallb[0:1, 3 * S:3 * S + 8], sprop, -1.0, None, OP.mult), "sV")
        v_nf = run("vector", lambda h: h.tensor_scalar(
            smallf[0:1, 16:24], sprop, -1.0, None, OP.mult), "sV")
        wait("vector", ("sV", v_nf))
        v_nl = run("vector", lambda h: h.tensor_tensor(
            smallb[0:1, 3 * S + 8:3 * S + 16], smallf[0:1, 16:24],
            smallb[0:1, 3 * S:3 * S + 8], OP.subtract), "sV")

        tr_done, seg_done = [], []
        tr_mm = 0
        tmp_ops = [(m, qb) for m in range(DC) for qb in range(NB)]
        tmp_ch = -(-len(tmp_ops) // TC)
        tmp_done = {}
        for t in range(TC):
            pspi = 2 + (t % 2)
            psP = ps[pspi][:, 0:8]
            wait("tensor", ("sV", v_nl), ("sV", xlos[-1]),
                 ("sV", seg_done[t - 2] if t >= 2 else None))
            first = True
            for src, pr in ((A, prot_hi), (A, prot_lo), (Bb, prot_hi), (Bb, prot_lo)):
                for m in range(DC):
                    run("tensor", lambda h, src=src, pr=pr, m=m, t=t, psP=psP,
                        first=first: h.matmul(
                        psP, til(src, m, t * 128, t * 128 + 128),
                        pr[:, m * 8:(m + 1) * 8], start=first, stop=False), None)
                    first = False
            for i, (mo, nr) in enumerate(((S, 0), (S, 8), (2 * S, 0),
                                          (2 * S, 8))):
                run("tensor", lambda h, mo=mo, nr=nr, t=t, psP=psP, i=i: h.matmul(
                    psP, smallb[0:1, mo + t * 128:mo + t * 128 + 128],
                    smallb[0:1, 3 * S + nr:3 * S + nr + 8],
                    start=False, stop=(i == 3)),
                    "sT" if i == 3 else None)
            mm = cnt["sT"]
            wait("vector", ("sT", mm))
            v1 = run("vector", lambda h, psP=psP: h.tensor_reduce(
                mx8[:, 0:1], psP, AX.X, OP.max), "sV")
            wait("vector", ("sV", v1))
            v2 = run("vector", lambda h, psP=psP: h.max_index(
                segu[:], bass.AP(mx8[:].tensor, 0, [mx8[:].ap[0], [0, 8]]), psP), "sV")
            wait("vector", ("sV", v2))
            v3 = run("vector", lambda h, t=t: h.tensor_copy(
                segidN[:, t:t + 1], segu[:, 0:1]), "sV")
            wait("vector", ("sV", v3), ("sT", tr_mm if t >= 1 else None))
            v4 = run("vector", lambda h, t=t: h.tensor_scalar(
                ohN[:], iotaF8[:], segidN[:, t:t + 1], None, OP.is_equal), "sV")
            seg_done.append(v4)
            for m2, qb2 in tmp_ops[t * tmp_ch:(t + 1) * tmp_ch]:
                wait("vector", ("sT", mm_mean))
                run("vector", lambda h, m=m2, qb=qb2: h.tensor_tensor(
                    til(Cb, m, qb * 512, qb * 512 + 512),
                    til(Bb, m, qb * 512, qb * 512 + 512), psM(qb),
                    OP.subtract), "sV")
                tmp_done[m2] = cnt["sV"]
            psT_sl = ps[pspi][0:8, 640:768]
            wait("tensor", ("sV", v4), ("sV", v_const),
                 ("sV", tr_done[t - 2] if t >= 2 else None))
            tr_mm = run("tensor", lambda h, psT_sl=psT_sl: h.transpose(
                psT_sl, ohN[:], identf[:]), "sT")
            wait("vector", ("sT", tr_mm))
            v5 = run("vector", lambda h, t=t, psT_sl=psT_sl: h.tensor_copy(
                smallb[0:8, t * 128:(t + 1) * 128], psT_sl), "sV")
            tr_done.append(v5)
        mm_ps_last = cnt["sT"]

        wait("vector", ("sV", tr_done[-1]))
        v_cnt = run("vector", lambda h: h.tensor_reduce(
            cntf[:], smallb[0:8, 0:S], AX.X, OP.add), "sV")
        wait("vector", ("sV", v_cnt))
        v_cr = run("vector", lambda h: h.tensor_scalar(
            cnt_rep[:], ones_bf[0:8, 0:128], cntf[0:8, 0:1], None, OP.mult), "sV")
        wait("tensor", ("sV", v_cr))
        for qb in range(NB):
            run("tensor", lambda h, qb=qb: h.matmul(
                psq23(qb), cnt_rep[:], smallb[0:8, qb * 512:qb * 512 + 512],
                start=True, stop=True), "sT")
        mm_tc = cnt["sT"]
        wait("scalar", ("sT", mm_tc))
        for qb in range(NB):
            run("scalar", lambda h, qb=qb: h.activation(
                is1B[:, qb * 512:qb * 512 + 512], psq23(qb), AF.Relu,
                scale=f32(-1.0), bias=f32(2.0)), "sS")
        s_is1 = cnt["sS"]
        wait("tensor", ("sS", s_is1))
        for qb in range(NB):
            run("tensor", lambda h, qb=qb: h.matmul(
                psq23(qb)[0:1, :], iota8c[:], smallb[0:8, qb * 512:qb * 512 + 512],
                start=True, stop=True), "sT")
        mm_sg = cnt["sT"]
        wait("vector", ("sT", mm_sg))
        for qb in range(NB):
            run("vector", lambda h, qb=qb: h.tensor_copy(
                smallb[0:1, S + qb * 512:S + qb * 512 + 512], psq23(qb)[0:1, :]),
                "sV")
        v_sgT = cnt["sV"]
        wait("tensor", ("sV", v_sgT))
        for qb in range(NB):
            run("tensor", lambda h, qb=qb: h.matmul(
                psq23(qb), ones_bf[0:1, 0:128],
                smallb[0:1, S + qb * 512:S + qb * 512 + 512],
                start=True, stop=True), "sT")
        mm_sr = cnt["sT"]
        wait("vector", ("sT", mm_sr))
        for qb in range(NB):
            run("vector", lambda h, qb=qb: h.tensor_copy(
                segrep[:, qb * 512:qb * 512 + 512], psq23(qb)), "sV")
        seg_ready = cnt["sV"]

        cbf_done = []
        for m in range(DC):
            wait("vector", ("sT", mm_ps_last), ("sV", tmp_done[m]))
            vc = run("vector", lambda h, m=m: h.tensor_tensor(
                til(Bb, m), til(A, m), til(Cb, m), OP.add), "sV")
            cbf_done.append(vc)
        sq_done = []
        for m in range(DC):
            wait("scalar", ("sV", cbf_done[m]))
            sc = run("scalar", lambda h, m=m: h.activation(
                til(Cb, m), til(Bb, m), AF.Square), "sS")
            sq_done.append(sc)
        wait("tensor", ("sV", tmp_done[DC - 1]), ("sV", v_ml))
        for qb in range(NB):
            for m in range(DC):
                wait("tensor", ("sS", sq_done[m]))
                run("tensor", lambda h, qb=qb, m=m: h.matmul(
                    psM(qb), ones_sc[:], til(Cb, m, qb * 512, qb * 512 + 512),
                    start=(m == 0), stop=(m == DC - 1)),
                    "sT" if m == DC - 1 else None)
        mm_var = cnt["sT"]
        wait("scalar", ("sT", mm_var))
        for qb in range(NB):
            run("scalar", lambda h, qb=qb: h.activation(
                xbig[:, qb * 512:qb * 512 + 512], psM(qb), AF.Ln,
                bias=f32(1e-5)), "sS")
        s_ln = cnt["sS"]
        rstd_qb = []
        wait("scalar", ("sS", s_ln))
        for qb in range(NB):
            run("scalar", lambda h, qb=qb: h.activation(
                xbig[:, S + qb * 512:S + qb * 512 + 512],
                xbig[:, qb * 512:qb * 512 + 512], AF.Exp,
                scale=f32(-0.5)), "sS")
            rstd_qb.append(cnt["sS"])
        s_sq = cnt["sS"]
        ln_piece = [0] * NB
        for qb in range(NB):
            wait("vector", ("sS", rstd_qb[qb]))
            for m in range(DC):
                run("vector", lambda h, m=m, qb=qb: h.tensor_tensor(
                    til(A, m, qb * 512, qb * 512 + 512),
                    til(Bb, m, qb * 512, qb * 512 + 512),
                    rstdB[:, qb * 512:qb * 512 + 512], OP.mult), "sV")
            ln_piece[qb] = cnt["sV"]
        ln_ready = cnt["sV"]

        if debug:
            wait("gpsimd", ("sV", ln_ready), ("sS", s_is1), ("sV", seg_ready))
            for m in range(DC):
                run("gpsimd", lambda h, m=m: h.dma_start(
                    dbg["d_ln1"][m * 128:(m + 1) * 128, :], til(A, m)), "dG", 16)
            run("gpsimd", lambda h: h.dma_start(
                bass.AP(dbg["d_seg"].tensor, 0, [[1, 128], [128, TC]]),
                segidN[:, 0:TC], allow_slow_non_contiguous=True), "dG", 16)
            run("vector", lambda h: h.tensor_copy(
                e0[0:1, 0:S], is1B[0:1, 0:S]), "sV")
            wait("gpsimd", ("sV", cnt["sV"]))
            run("gpsimd", lambda h: h.dma_start(
                bass.AP(dbg["d_is1"].tensor, 0, [[1, S]]), e0[0:1, 0:S]), "dG")

        attn_state = {"ps01_rd": 0,
                      "norm_done": 0}

        def mha(wqk, wv, bq, bk, bo_col, bv_col, wob_col, wout, masked):
            pair_done_sV = 0
            last_select = 0
            for hv in range(NV):
                if attn_state.get("tsc") is not None:
                    attn_state.pop("tsc")()
                if attn_state.get("pt") is not None:
                    attn_state.pop("pt")()
                VW64 = VHH * 64
                use_qk_stage = hv == 0 and DC * VW64 <= 2 * S
                if use_qk_stage:
                    npb = S // VW64

                    def vst_ap(m):
                        buf = qg if m < npb else kg
                        return buf[:, (m % npb) * VW64:(m % npb) * VW64 + VW64]
                    dep_rd = ("sT", max(attn_state.get("mms_sl", [0, 0])))
                    dep_wr = ("sV", attn_state.get("qkev", 0))
                    vw = 0
                    for c in range(DC):
                        so = (c % 4) * 512
                        ws = wstate["wst"][c % 4]
                        wait("sync", ws if isinstance(ws, tuple) else None)
                        dv = run("sync", lambda h, c=c, so=so, hv=hv:
                                 h.dma_start(
                            bass.AP(wst[:].tensor, so,
                                    [wst[:].ap[0], [VW64, 1], [1, VW64]]),
                            bass.AP(wv.tensor,
                                    c * 128 * (DH * H) + hv * VW64,
                                    [[DH * H, 128], [128 * (DH * H), 1],
                                     [1, VW64]])), "dS")
                        wait("gpsimd", dv, dep_rd, dep_wr)
                        v = run("gpsimd", lambda h, ap=vst_ap(c), so=so:
                                h.tensor_copy(
                            ap, wst[:, so:so + VW64]), "sG")
                        wstate["wst"][c % 4] = ("sG", v)
                        vw = ("sG", v)
                    if debug:
                        wait("gpsimd", vw)
                        run("gpsimd", lambda h: h.dma_start(
                            dbg["d_vq"][:, :], qg[:, 0:S]), "dG")
                        vw = ("sG", cnt["sG"])
                        run("gpsimd", lambda h: h.dma_start(
                            dbg["d_vk"][:, :], kg[:, 0:S]), "dG")
                else:
                    def vst_ap(m):
                        return wbf[:, m * 512:m * 512 + VW64]
                    vw = 0
                    for c in range(DC):
                        vw = load_cols(wv, DH * H, hv * VW64, VW64,
                                       c % 4, c * 512, nchunk_lo=c,
                                       nchunk_hi=c + 1)
                vev = []
                wait("tensor", vw,
                     ("sV", attn_state["norm_done"]),
                     attn_state.get("or_done"))
                for t in range(TC):
                    pp = ps[2 + (t % 2)][:, 0:VHH * 64]
                    wait("tensor", ("sV", vev[t - 2] if t >= 2 else None),
                         ("sV", ln_piece[min(t // 4, NB - 1)]))
                    for m in range(DC):
                        run("tensor", lambda h, t=t, m=m, pp=pp,
                            vap=vst_ap(m): h.matmul(
                            pp, til(A, m, t * 128, t * 128 + 128),
                            vap,
                            start=(m == 0), stop=(m == DC - 1)),
                            "sT" if m == DC - 1 else None)
                    mmv = cnt["sT"]
                    if not use_qk_stage:
                        mark_wbf(0, 4096, mmv)
                    wait("vector", ("sT", mmv))
                    run("vector", lambda h, t=t, pp=pp: h.tensor_copy(
                        bass.AP(vg[:].tensor, t * VW,
                                [vg[:].ap[0], [161, VHH // 2], [1, 64]]),
                        bass.AP(pp.tensor, pp.offset,
                                [pp.ap[0], [128, VHH // 2], [1, 64]])), "sV")
                    ve = run("vector", lambda h, t=t, pp=pp: h.tensor_copy(
                        bass.AP(vg[:].tensor, t * VW + 97,
                                [vg[:].ap[0], [161, VHH // 2], [1, 64]]),
                        bass.AP(pp.tensor, pp.offset + 64,
                                [pp.ap[0], [128, VHH // 2], [1, 64]])), "sV")
                    vev.append(ve)
                v_ready = vev[-1]

                for p in range(hv * VHH // 2, (hv + 1) * VHH // 2):
                    wslot = (p + 1) % 2 if masked else p % 2
                    staged = attn_state.get("nx_p") == p
                    kstaged = staged and attn_state.get("nx_kinj")
                    kbuf = qg if (p % 2 == 0 and p >= 2) else kg
                    if staged:
                        attn_state.pop("nx_p")
                        attn_state.pop("nx_cq")
                        attn_state.pop("nx_kinj")
                        qgev = attn_state.pop("stq_ev")
                        stk_ev = attn_state.pop("stk_ev")
                        stqbuf = attn_state.pop("nx_buf")
                        ck = attn_state.pop("nx_ck")
                        attn_state.pop("nx_wsl")
                        if ck is None:
                            for half in (0, 1):
                                ck = load_cols(
                                    wqk, FF, p * 256 + 128, 128,
                                    2 * wslot + 1,
                                    (2 * wslot + 1) * 1024 + half * 512,
                                    nchunk_lo=half * 4,
                                    nchunk_hi=half * 4 + 4)

                        def qap(r0, r1, c0, w, stqbuf=stqbuf, p=p):
                            return stqbuf[r0:r1, p * S + c0:p * S + c0 + w]
                    else:
                        cq = ck = 0
                        for j in (0, 1):
                            for half in (0, 1):
                                c = load_cols(wqk, FF, p * 256 + j * 128, 128,
                                              2 * wslot + j,
                                              (2 * wslot + j) * 1024 + half * 512,
                                              nchunk_lo=half * 4,
                                              nchunk_hi=half * 4 + 4)
                            if j == 0:
                                cq = c
                            else:
                                ck = c

                        def qap(r0, r1, c0, w):
                            return qg[r0:r1, c0:c0 + w]
                    if p + 1 < H // 2:
                        pn = p + 1
                        kinj = pn >= 2
                        same_hv = pn < (hv + 1) * VHH // 2
                        wsl_n = (pn + 1) % 2 if masked else pn % 2
                        cq2 = 0
                        ck2 = None
                        for j in ((0, 1) if (same_hv or kinj) else (0,)):
                            for half in (0, 1):
                                c = load_cols(wqk, FF, pn * 256 + j * 128, 128,
                                              2 * wsl_n + j,
                                              (2 * wsl_n + j) * 1024 + half * 512,
                                              nchunk_lo=half * 4,
                                              nchunk_hi=half * 4 + 4)
                            if j == 0:
                                cq2 = c
                            else:
                                ck2 = c
                        attn_state["nx_p"] = pn
                        attn_state["nx_cq"] = cq2
                        attn_state["nx_ck"] = ck2
                        attn_state["nx_kinj"] = kinj
                        attn_state["nx_wsl"] = wsl_n
                        attn_state["nx_buf"] = Bb if masked else Cb
                        attn_state["stq_ev"] = [0] * NB
                        attn_state["stk_ev"] = [0] * NB

                    def stage_q_part(j, pc, bq=bq):
                        par1 = 1 - (j % 2)
                        pst = ps[2 + par1]
                        wsl_n = attn_state["nx_wsl"]
                        pn = attn_state["nx_p"]
                        if pc == 0:
                            wait("tensor", attn_state["nx_cq"],
                                 ("sV", attn_state.get("avp%d" % par1, 0)))
                        for m in (2 * pc, 2 * pc + 1):
                            run("tensor", lambda h, m=m, j=j, pst=pst,
                                wsl_n=wsl_n: h.matmul(
                                pst[:, 0:512],
                                wbf[:, 2 * wsl_n * 1024 + m * 128:
                                    2 * wsl_n * 1024 + m * 128 + 128],
                                til(A, m, j * 512, j * 512 + 512),
                                start=(m == 0), stop=(m == DC - 1)),
                                "sT" if m == DC - 1 else None)
                        if pc < DC // 2 - 1:
                            return
                        g = cnt["sT"]
                        mark_wbf(2 * wsl_n * 1024, 2 * wsl_n * 1024 + 1024, g)
                        buf = attn_state["nx_buf"]
                        wait("vector", ("sT", g))
                        ev = run("vector", lambda h, j=j, pst=pst, buf=buf,
                                 pn=pn, bq=bq: h.tensor_scalar(
                            buf[:, pn * S + j * 512:pn * S + j * 512 + 512],
                            pst[:, 0:512],
                            bias_sb[:, bq * 8 + pn:bq * 8 + pn + 1], 0.125,
                            OP.add, OP.mult), "sV")
                        attn_state["stq_ev"][j] = ev
                        attn_state["avp%d" % par1] = ev

                    def stage_k_part(j, pc, bk=bk):
                        par1 = 1 - (j % 2)
                        pst = ps[2 + par1]
                        wsl_n = attn_state["nx_wsl"]
                        pn = attn_state["nx_p"]
                        if pc == 0:
                            wait("tensor", attn_state["nx_ck"],
                                 ("sV", attn_state.get("avp%d" % par1, 0)))
                        for m in (2 * pc, 2 * pc + 1):
                            run("tensor", lambda h, m=m, j=j, pst=pst,
                                wsl_n=wsl_n: h.matmul(
                                pst[:, 512:1024],
                                wbf[:, (2 * wsl_n + 1) * 1024 + m * 128:
                                    (2 * wsl_n + 1) * 1024 + m * 128 + 128],
                                til(A, m, j * 512, j * 512 + 512),
                                start=(m == 0), stop=(m == DC - 1)),
                                "sT" if m == DC - 1 else None)
                        if pc < DC // 2 - 1:
                            return
                        g = cnt["sT"]
                        mark_wbf((2 * wsl_n + 1) * 1024,
                                 (2 * wsl_n + 1) * 1024 + 1024, g)
                        pn_kbuf = qg if pn % 2 == 0 else kg
                        wait("vector", ("sT", g))
                        ev = run("vector", lambda h, j=j, pst=pst,
                                 kb=pn_kbuf, pn=pn, bk=bk: h.tensor_scalar(
                            kb[:, j * 512:j * 512 + 512],
                            pst[:, 512:1024],
                            bias_sb[:, bk * 8 + pn:bk * 8 + pn + 1], None,
                            OP.add), "sV")
                        attn_state["stk_ev"][j] = ev
                        attn_state["avp%d" % par1] = ev

                    if attn_state.get("tsc") is not None:
                        attn_state.pop("tsc")()
                    if attn_state.get("pt") is not None:
                        attn_state.pop("pt")()
                    if kstaged:
                        kgev = stk_ev
                    elif staged:
                        wait("tensor", ck, ("sV", v_ready),
                             ("sV", attn_state["norm_done"]),
                             attn_state.get("or_done"),
                             ("sV", attn_state.get("avp0", 0)),
                             ("sS", st_exp[0]))
                    else:
                        wait("tensor", ck, cq, ("sV", v_ready),
                             ("sV", attn_state["norm_done"]),
                             attn_state.get("or_done"))
                    if not kstaged:
                        kgev = []
                    if not staged:
                        qgev = []
                    for qb in range(NB if not kstaged else 0):
                        if staged:
                            psk = (ps[2][:, (qb % 2) * 512:(qb % 2) * 512 + 512]
                                   if qb < 2 else
                                   ps[0][:, (qb % 2) * 512:(qb % 2) * 512 + 512])
                        else:
                            psq = ps[qb // 2][:, (qb % 2) * 512:
                                              (qb % 2) * 512 + 512]
                            psk = psq23(qb)
                            for m in range(DC):
                                run("tensor", lambda h, qb=qb, m=m, psq=psq,
                                    wslot=wslot: h.matmul(
                                    psq, wbf[:, 2 * wslot * 1024 + m * 128:
                                             2 * wslot * 1024 + m * 128 + 128],
                                    til(A, m, qb * 512, qb * 512 + 512),
                                    start=(m == 0), stop=(m == DC - 1)),
                                    "sT" if m == DC - 1 else None)
                        for m in range(DC):
                            run("tensor", lambda h, qb=qb, m=m, psk=psk, wslot=wslot:
                                h.matmul(
                                psk, wbf[:, (2 * wslot + 1) * 1024 + m * 128:
                                         (2 * wslot + 1) * 1024 + m * 128 + 128],
                                til(A, m, qb * 512, qb * 512 + 512),
                                start=(m == 0), stop=(m == DC - 1)),
                                "sT" if m == DC - 1 else None)
                        gk = cnt["sT"]
                        wait("vector", ("sT", gk))
                        if debug and qb == 0:
                            wait("vector", *all_lanes("dG"))
                        run("vector", lambda h, qb=qb, psk=psk, p=p,
                            kbuf=kbuf: h.tensor_scalar(
                            kbuf[:, qb * 512:qb * 512 + 512], psk,
                            bias_sb[:, bk * 8 + p:bk * 8 + p + 1], None,
                            OP.add), "sV")
                        kgev.append(cnt["sV"])
                        if not staged:
                            run("vector", lambda h, qb=qb, psq=psq, p=p:
                                h.tensor_scalar(
                                qg[:, qb * 512:qb * 512 + 512], psq,
                                bias_sb[:, bq * 8 + p:bq * 8 + p + 1], 0.125,
                                OP.add, OP.mult), "sV")
                            qgev.append(cnt["sV"])
                    mmqk = cnt["sT"]
                    mark_wbf((2 * wslot + 1) * 1024,
                             (2 * wslot + 1) * 1024 + 1024, mmqk)
                    if not staged:
                        mark_wbf(2 * wslot * 1024, 2 * wslot * 1024 + 1024,
                                 mmqk)
                    qk_ready = cnt["sV"]
                    attn_state["qkev"] = qk_ready
                    if debug and p == 0 and not masked:
                        wait("gpsimd", ("sV", qk_ready))
                        run("gpsimd", lambda h: h.dma_start(
                            dbg["d_qg"][:, :], qg[:, 0:S]), "dG")
                        run("gpsimd", lambda h: h.dma_start(
                            dbg["d_kg"][:, :], kg[:, 0:S]), "dG")

                    stage_kcs = ([4, 7, 10, 13] if TC >= 14
                                 else [3, 4, 5, 6])[:DC // 2]
                    stagek_kcs = ([5, 8, 11, 14] if TC >= 15
                                  else [3, 4, 5, 6])[:DC // 2]
                    hl0 = (2 * p) % VHH
                    jr = hl0 // 2
                    st_exp = attn_state.setdefault("exp_sl", [0, 0])
                    st_av = attn_state.setdefault("av_sl", [0, 0])
                    st_mult = attn_state.setdefault("mult_sl", [0, 0])
                    st_mms = attn_state.setdefault("mms_sl", [0, 0])
                    st_mgen = attn_state.setdefault("mgen_sl", [0, 0])
                    st_exp4 = attn_state.setdefault("exp4_sl", [0, 0, 0, 0])
                    st_av4 = attn_state.setdefault("av4_sl", [0, 0, 0, 0])

                    def arena(kc):
                        s4 = kc % 4
                        buf = e0 if s4 < 2 else me0
                        return s4, buf, (s4 % 2) * 1024
                    for q4 in range(S // 512):
                        c0 = q4 * 512
                        par = (p * (S // 512) + q4) % 2
                        psA = ps[2 + par]

                        def emit_scores(kc, c0=c0, q4=q4):
                            sl = kc % 2
                            if staged:
                                stg = ([("sV", kgev[qb]) for qb in (2, 3)
                                        if qb < NB] if sl == 0 else [])
                            else:
                                stg = [("sV", qgev[qb])
                                       for qb in (2 * sl, 2 * sl + 1)
                                       if qb < NB]
                            wait("tensor", ("sV", kgev[kc // 4]),
                                 ("sV", qgev[q4]),
                                 ("sS", st_exp[sl]), *stg)
                            run("tensor", lambda h, kc=kc, kbuf=kbuf,
                                qa=qap(0, 64, c0, 512): h.matmul(
                                ps[sl][:, 0:512],
                                kbuf[0:64, kc * 128:kc * 128 + 128],
                                qa,
                                start=True, stop=True), None)
                            st_mms[sl] = run("tensor", lambda h, kc=kc,
                                             kbuf=kbuf,
                                             qa=qap(64, 128, c0, 512):
                                             h.matmul(
                                ps[sl][:, 512:1024],
                                kbuf[64:128, kc * 128:kc * 128 + 128],
                                qa,
                                start=True, stop=True), "sT")

                        def emit_mask(kc, c0=c0):
                            sl = kc % 2
                            wait("vector", ("sV", seg_ready),
                                 ("sV", st_mult[sl]))
                            st_mgen[sl] = run("vector", lambda h, kc=kc, sl=sl,
                                              c0=c0: h.tensor_scalar(
                                maskt[:, sl * 512:sl * 512 + 512],
                                segrep[:, c0:c0 + 512],
                                segidN[:, kc:kc + 1], None, OP.is_equal), "sV")

                        def emit_exp(kc):
                            sl = kc % 2
                            if masked:
                                wait("scalar", ("sT", st_mms[sl]),
                                     ("sT", st_av4[sl]),
                                     ("sV", st_mult[sl]))
                                dst = e0[:, sl * 1024:sl * 1024 + 1024]
                            else:
                                s4, buf, off = arena(kc)
                                wait("scalar", ("sT", st_mms[sl]),
                                     ("sT", st_av4[s4]))
                                dst = buf[:, off:off + 1024]
                            v = run("scalar", lambda h, sl=sl, dst=dst:
                                    h.activation(
                                dst, ps[sl][:, 0:1024], AF.Exp,
                                bias=f32(-28.0)), "sS")
                            st_exp[sl] = v
                            if not masked:
                                st_exp4[kc % 4] = v

                        def emit_mult(kc):
                            sl = kc % 2
                            wait("vector", ("sS", st_exp[sl]),
                                 ("sT", st_av[sl]), ("sT", st_av4[2 + sl]),
                                 ("sV", st_mgen[sl]))
                            mrep = bass.AP(maskt[:].tensor, sl * 512,
                                           [maskt[:].ap[0], [0, 2], [1, 512]])
                            st_mult[sl] = run("vector", lambda h, sl=sl,
                                              mrep=mrep: h.tensor_tensor(
                                me0[:, sl * 1024:sl * 1024 + 1024],
                                e0[:, sl * 1024:sl * 1024 + 1024],
                                mrep, OP.mult), "sV")

                        def emit_av(kc, psA=psA, par=par):
                            sl = kc % 2
                            if masked:
                                srcb = me0
                                soff = sl * 1024
                                wait("tensor", ("sV", st_mult[sl]))
                            else:
                                s4, srcb, soff = arena(kc)
                                wait("tensor", ("sS", st_exp4[s4]))
                            if kc == 0:
                                stg = [("sV", kgev[qb])
                                       for qb in (2 * par, 2 * par + 1)
                                       if qb < NB]
                                wait("tensor",
                                     ("sV", attn_state.get("avp%d" % par, 0)),
                                     attn_state.get("orp%d" % par),
                                     ("sS", attn_state.get("lnp%d" % par, 0)),
                                     ("sV", v_ready),
                                     *stg)
                            run("tensor", lambda h, kc=kc, soff=soff, srcb=srcb,
                                jr=jr, psA=psA: h.matmul(
                                psA[0:65, 0:512],
                                vg[:, kc * VW + jr * 161:kc * VW + jr * 161 + 65],
                                srcb[:, soff:soff + 512],
                                start=(kc == 0), stop=(kc == TC - 1)), None)
                            avm = run("tensor", lambda h, kc=kc, soff=soff,
                                      srcb=srcb, jr=jr, psA=psA: h.matmul(
                                psA[:, 512:1024],
                                vg[:, kc * VW + jr * 161 + 33:
                                   kc * VW + jr * 161 + 161],
                                srcb[:, soff + 512:soff + 1024],
                                start=(kc == 0), stop=(kc == TC - 1)), "sT")
                            st_av[sl] = avm
                            if not masked:
                                st_av4[kc % 4] = avm

                        emit_scores(0)
                        if masked:
                            emit_mask(0)
                        emit_exp(0)
                        if masked:
                            emit_mult(0)
                        if TC > 1:
                            emit_scores(1)
                            if masked:
                                emit_mask(1)
                        for kc in range(TC):
                            if kc + 1 < TC:
                                emit_exp(kc + 1)
                                if masked:
                                    emit_mult(kc + 1)
                            if kc + 2 < TC:
                                emit_scores(kc + 2)
                                if masked:
                                    emit_mask(kc + 2)
                            if kc == 0 and attn_state.get("tsc") is not None:
                                attn_state.pop("tsc")()
                            if kc == 1 and attn_state.get("pt") is not None:
                                attn_state.pop("pt")()
                            if (attn_state.get("nx_cq") is not None
                                    and kc in stage_kcs):
                                stage_q_part(q4, stage_kcs.index(kc))
                            if (attn_state.get("nx_kinj")
                                    and kc in stagek_kcs):
                                stage_k_part(q4, stagek_kcs.index(kc))
                            emit_av(kc)
                        av_last = cnt["sT"]

                        def tail_sc(psA=psA, par=par, av_last=av_last):
                            wait("scalar", ("sT", av_last),
                                 ("sT", attn_state.get("rep_done", 0)),
                                 ("sV", attn_state.get("avp%d" % par, 0)))
                            run("scalar", lambda h, psA=psA: h.activation(
                                xst[64:65, 1024:1536], psA[64:65, 0:512],
                                AF.Ln), "sS")
                            run("scalar", lambda h, psA=psA: h.activation(
                                xst[32:33, 1536:2048], psA[32:33, 512:1024],
                                AF.Ln), "sS")
                            wait("scalar", ("sS", cnt["sS"]))
                            run("scalar", lambda h: h.activation(
                                recb2[64:65, 0:512], xst[64:65, 1024:1536],
                                AF.Exp, scale=f32(-1.0)), "sS")
                            v_rec = run("scalar", lambda h: h.activation(
                                recb2[32:33, 512:1024], xst[32:33, 1536:2048],
                                AF.Exp, scale=f32(-1.0)), "sS")
                            attn_state["lnp%d" % par] = v_rec
                            attn_state["v_rec_last"] = v_rec
                        attn_state["tsc"] = tail_sc
                        wait("vector", ("sT", av_last),
                             ("sV", attn_state.get("avp%d" % par, 0)))
                        run("vector", lambda h, psA=psA, par=par: h.tensor_copy(
                            outraw[0:64, par * 512:par * 512 + 512],
                            psA[0:64, 0:512]), "sV")
                        s_or = run("vector", lambda h, psA=psA, par=par:
                                   h.tensor_copy(
                            outraw[64:128, par * 512:par * 512 + 512],
                            psA[64:128, 512:1024]), "sV")
                        attn_state["orp%d" % par] = ("sV", s_or)
                        attn_state["or_done"] = ("sV", s_or)

                        def tail_post(psA=psA, par=par, p=p, c0=c0, s_or=s_or,
                                      v_rec_=None):
                            wait("tensor", ("sV", s_or),
                                 ("sS", attn_state["v_rec_last"]))
                            run("tensor", lambda h, psA=psA: h.matmul(
                                psA[0:64, 0:512], onesf[64:65, 0:64],
                                recb2[64:65, 0:512], start=True, stop=True),
                                None)
                            mm_rep = run("tensor", lambda h, psA=psA: h.matmul(
                                psA[64:128, 512:1024], onesf[32:33, 0:64],
                                recb2[32:33, 512:1024], start=True, stop=True),
                                "sT")
                            attn_state["rep_done"] = mm_rep
                            wait("vector", ("sT", mm_rep))
                            run("vector", lambda h, p=p, c0=c0, psA=psA,
                                par=par: h.tensor_tensor(
                                Bb[0:64, p * S + c0:p * S + c0 + 512],
                                outraw[0:64, par * 512:par * 512 + 512],
                                psA[0:64, 0:512], OP.mult), "sV")
                            run("vector", lambda h, p=p, c0=c0, psA=psA,
                                par=par: h.tensor_tensor(
                                Bb[64:128, p * S + c0:p * S + c0 + 512],
                                outraw[64:128, par * 512:par * 512 + 512],
                                psA[64:128, 512:1024], OP.mult), "sV")
                            attn_state["norm_done"] = cnt["sV"]
                            attn_state["avp%d" % par] = cnt["sV"]
                        attn_state["pt"] = tail_post
            if attn_state.get("tsc") is not None:
                attn_state.pop("tsc")()
            if attn_state.get("pt") is not None:
                attn_state.pop("pt")()
            outT_ready_sV = attn_state["norm_done"]
            if debug and masked:
                sl_ = (TC - 1) % 2
                wait("gpsimd", ("sV", outT_ready_sV))
                run("gpsimd", lambda h, sl_=sl_: h.dma_start(
                    dbg["d_mask"][:, 0:512], maskt[:, sl_ * 512:sl_ * 512 + 512]),
                    "dG")
                run("gpsimd", lambda h, sl_=sl_: h.dma_start(
                    dbg["d_me"][:, 0:1024], me0[:, sl_ * 1024:sl_ * 1024 + 1024]),
                    "dG")
            if debug:
                nm_ = "d_ba" if not masked else "d_bs"
                wait("gpsimd", ("sV", outT_ready_sV))
                for m in range(HC):
                    run("gpsimd", lambda h, m=m, nm_=nm_: h.dma_start(
                        dbg[nm_][m * 128:(m + 1) * 128, :], til(Bb, m)), "dG")

            sel_last = 0
            if not masked:
                def wout_load(mb):
                    wslot = mb % 2
                    cw = 0
                    nh = min(4, HC)
                    for half in range(HC // nh):
                        cw = load_cols(wout, D, mb * 128, 128, 2 * wslot + half,
                                       wslot * 1024 + half * nh * 128,
                                       nchunk_lo=half * nh,
                                       nchunk_hi=(half + 1) * nh)
                    return cw
                cw = wout_load(0)
                for mb in range(DC):
                    wslot = mb % 2
                    nx = wout_load(mb + 1) if mb + 1 < DC else None
                    wait("tensor", cw, ("sV", outT_ready_sV),
                         ("sV", attn_state.get("wob_v", 0)),
                         attn_state.get("or_done"))
                    for c in range(HC):
                        run("tensor", lambda h, c=c, wslot=wslot, bv_col=bv_col:
                            h.matmul(
                            ps[2][:, 512:513],
                            wbf[:, wslot * 1024 + c * 128:wslot * 1024 + c * 128 + 128],
                            bvb[:, bv_col * 8 + c:bv_col * 8 + c + 1],
                            start=(c == 0), stop=(c == HC - 1)),
                            "sT" if c == HC - 1 else None)
                    mm_fold = cnt["sT"]
                    wait("vector", ("sT", mm_fold))
                    v_wob = run("vector", lambda h, mb=mb: h.tensor_scalar(
                        wob[:, wob_col * 8 + mb:wob_col * 8 + mb + 1],
                        ps[2][:, 512:513],
                        bias_sb[:, bo_col * 8 + mb:bo_col * 8 + mb + 1],
                        None, OP.add), "sV")
                    attn_state["wob_v"] = v_wob
                    for qc in range(NB):
                        wpar = (mb * NB + qc) % 2
                        pw = ps[wpar][:, 0:512]
                        wevp = attn_state.setdefault("wout_evp", {0: 0, 1: 0})
                        wait("tensor", ("sV", wevp[wpar]))
                        for c in range(HC):
                            run("tensor", lambda h, c=c, qc=qc, pw=pw, wslot=wslot:
                                h.matmul(
                                pw, wbf[:, wslot * 1024 + c * 128:
                                        wslot * 1024 + c * 128 + 128],
                                til(Bb, c, qc * 512, qc * 512 + 512),
                                start=(c == 0), stop=(c == HC - 1)),
                                "sT" if c == HC - 1 else None)
                        mmw = cnt["sT"]
                        mark_wbf(wslot * 1024, wslot * 1024 + 1024, mmw)
                        wait("vector", ("sT", mmw), ("sV", v_wob))
                        wevp[wpar] = run(
                            "vector", lambda h, mb=mb, qc=qc, pw=pw: h.tensor_scalar(
                                til(Cb, mb, qc * 512, qc * 512 + 512), pw,
                                wob[:, wob_col * 8 + mb:wob_col * 8 + mb + 1],
                                None, OP.add), "sV")
                    cw = nx
            else:
                def _ws(pool, need):
                    out = []
                    for t, w in pool:
                        for o in range(0, w - 511, 512):
                            out.append((t, o))
                            if len(out) == need:
                                return out
                    raise RuntimeError("no wout staging")
                dstw = _ws([(vg, TC * VW), (e0, 2048), (me0, 2048),
                            (maskt, 1024), (segrep, S)], DC * NB)

                def dstw_ap(mb, qc):
                    t, o = dstw[mb * NB + qc]
                    return t[:, o:o + 512]

                def wout_load2(mb):
                    wslot = mb % 2
                    cw = 0
                    nh = min(4, HC)
                    for half in range(HC // nh):
                        cw = load_cols(wout, D, mb * 128, 128, 2 * wslot + half,
                                       wslot * 1024 + half * nh * 128,
                                       nchunk_lo=half * nh,
                                       nchunk_hi=(half + 1) * nh)
                    return cw
                cw = wout_load2(0)
                for mb in range(DC):
                    wslot = mb % 2
                    nx = wout_load2(mb + 1) if mb + 1 < DC else None
                    wait("tensor", cw, ("sV", outT_ready_sV),
                         ("sV", attn_state.get("wob_v", 0)),
                         attn_state.get("or_done"))
                    if debug:
                        wait("vector", *all_lanes("dG"))
                    for c in range(HC):
                        run("tensor", lambda h, c=c, wslot=wslot,
                            bv_col=bv_col: h.matmul(
                            ps[2][:, 512:513],
                            wbf[:, wslot * 1024 + c * 128:
                                wslot * 1024 + c * 128 + 128],
                            bvb[:, bv_col * 8 + c:bv_col * 8 + c + 1],
                            start=(c == 0), stop=(c == HC - 1)),
                            "sT" if c == HC - 1 else None)
                    mm_fold = cnt["sT"]
                    wait("vector", ("sT", mm_fold))
                    attn_state["wob_v"] = run(
                        "vector", lambda h, mb=mb: h.tensor_scalar(
                            wob[:, wob_col * 8 + mb:wob_col * 8 + mb + 1],
                            ps[2][:, 512:513],
                            bias_sb[:, bo_col * 8 + mb:
                                    bo_col * 8 + mb + 1],
                            None, OP.add), "sV")
                    for qc in range(NB):
                        wpar = (mb * NB + qc) % 2
                        pw = ps[wpar][:, 0:512]
                        wevp = attn_state.setdefault("wout_evp", {0: 0, 1: 0})
                        wait("tensor", ("sV", wevp[wpar]))
                        for c in range(HC):
                            run("tensor", lambda h, c=c, qc=qc, pw=pw, wslot=wslot:
                                h.matmul(
                                pw, wbf[:, wslot * 1024 + c * 128:
                                        wslot * 1024 + c * 128 + 128],
                                til(Bb, c, qc * 512, qc * 512 + 512),
                                start=(c == 0), stop=(c == HC - 1)),
                                "sT" if c == HC - 1 else None)
                        mmw = cnt["sT"]
                        mark_wbf(wslot * 1024, wslot * 1024 + 1024, mmw)
                        wait("vector", ("sT", mmw),
                             ("sV", attn_state.get("wob_v", 0)))
                        wevp[wpar] = run(
                            "vector", lambda h, mb=mb, qc=qc, pw=pw:
                            h.tensor_scalar(
                                dstw_ap(mb, qc), pw,
                                wob[:, wob_col * 8 + mb:wob_col * 8 + mb + 1],
                                None, OP.add), "sV")
                    cw = nx
                wout_mm_all = cnt["sT"]
                wait("vector", ("sS", s_is1), ("sT", wout_mm_all))
                for mb in range(DC):
                    for qc in range(NB):
                        run("vector", lambda h, mb=mb, qc=qc: h.select(
                            til(Bb, mb, qc * 512, qc * 512 + 512),
                            is1B[:, qc * 512:qc * 512 + 512],
                            til(A, mb, qc * 512, qc * 512 + 512),
                            dstw_ap(mb, qc),
                            add_drain=True), "sV")
                    sel_last = cnt["sV"]
            return cnt["sV"]

        lt_done = mha(wqk_lt, wv_lt, bq=0, bk=1, bo_col=3, bv_col=0, wob_col=0,
                      wout=wout_lt, masked=False)
        if debug:
            wait("gpsimd", ("sV", lt_done))
            for m in range(DC):
                run("gpsimd", lambda h, m=m: h.dma_start(
                    dbg["d_long"][m * 128:(m + 1) * 128, :], til(Cb, m)), "dG", 16)
        st_done = mha(wqk_st, wv_st, bq=4, bk=5, bo_col=7, bv_col=1, wob_col=1,
                      wout=wout_st, masked=True)
        if debug:
            wait("gpsimd", ("sV", st_done))
            for m in range(DC):
                run("gpsimd", lambda h, m=m: h.dma_start(
                    dbg["d_sel"][m * 128:(m + 1) * 128, :], til(Bb, m)), "dG", 16)

        if debug:
            wait("scalar", *all_lanes("dG"))
            wait("vector", *all_lanes("dG"))
        xslot_user = [st_done, st_done]
        wait("sync", ("sT", cnt["sT"]), ("sV", cnt["sV"]))
        x1_tile_dG = [None] * DC
        x1slot_user = [(0, None), (0, None)]
        fus_cast = []
        fus_mm_last = 0
        def fus_load(mb):
            wslot = mb % 2
            cw = 0
            for quarter in range(4):
                cw = load_cols(fuswT, D, mb * 128, 128, quarter,
                               wslot * 2048 + quarter * 512, eng="scalar",
                               nchunk_lo=quarter * 4, nchunk_hi=quarter * 4 + 4)
            return cw
        cwf = fus_load(0)
        for mb in range(DC):
            wslot = mb % 2
            nxf = fus_load(mb + 1) if mb + 1 < DC else None
            cw = cwf
            wait("tensor", cw, ("sV", st_done), ("sV", lt_done))
            for qc in range(NB):
                j = (mb * NB + qc) % 2
                wait("sync", ("sV", xslot_user[j]))
                dx = run("sync", lambda h, mb=mb, qc=qc, j=j: h.dma_start(
                    xst[:, j * 512:j * 512 + 512],
                    xT[mb * 128:(mb + 1) * 128, qc * 512:qc * 512 + 512]), "dS")
                pw = ps[(mb * NB + qc) % 2][:, 0:512]
                wait("tensor", ("sV", xslot_user[j]))
                for c in range(2 * DC):
                    src = Cb if c < DC else Bb
                    run("tensor", lambda h, c=c, qc=qc, pw=pw, wslot=wslot, src=src:
                        h.matmul(
                        pw, wbf[:, wslot * 2048 + c * 128:wslot * 2048 + c * 128 + 128],
                        til(src, c % DC, qc * 512, qc * 512 + 512),
                        start=(c == 0), stop=(c == 2 * DC - 1)),
                        "sT" if c == 2 * DC - 1 else None)
                mmf = cnt["sT"]
                fus_mm_last = mmf
                mark_wbf(wslot * 2048, wslot * 2048 + 2048, mmf)
                wait("vector", ("sT", mmf), dx,
                     ("sS", x1slot_user[j][0]), x1slot_user[j][1])
                vx1 = run("vector", lambda h, mb=mb, j=j, pw=pw: h.scalar_tensor_tensor(
                    xst[:, (2 + j) * 512:(2 + j) * 512 + 512], pw,
                    bias_sb[:, 64 + mb:64 + mb + 1],
                    xst[:, j * 512:j * 512 + 512], OP.add, OP.add), "sV")
                xslot_user[j] = vx1
                wait("scalar", ("sV", vx1))
                sc = run("scalar", lambda h, mb=mb, qc=qc, j=j: h.activation(
                    til(A, mb, qc * 512, qc * 512 + 512),
                    xst[:, (2 + j) * 512:(2 + j) * 512 + 512], AF.Copy), "sS")
                fus_cast.append(sc)
                wait("tensor", ("sS", sc),
                     ("sV", attn_state.get("wob_v", 0)) if mb == 0 else None,
                     ("sV", attn_state.get("avp0", 0)) if mb == 0 else None,
                     ("sV", attn_state.get("avp1", 0)) if mb == 0 else None)
                run("tensor", lambda h, mb=mb, qc=qc: h.matmul(
                    psq23(qc), ones_sc[:], til(A, mb, qc * 512, qc * 512 + 512),
                    start=(mb == 0), stop=(mb == DC - 1)),
                    "sT" if mb == DC - 1 else None)
                wait("gpsimd", ("sV", vx1))
                dg = run("gpsimd", lambda h, mb=mb, qc=qc, j=j: h.dma_start(
                    x1sc[mb * 128:(mb + 1) * 128, qc * 512:qc * 512 + 512],
                    xst[:, (2 + j) * 512:(2 + j) * 512 + 512]), "dG")
                x1slot_user[j] = (sc, dg)
                x1_tile_dG[mb] = dg
            cwf = nxf
        fus_cast_last = cnt["sS"]

        if debug:
            wait("gpsimd", *all_lanes("dG"))
            run("gpsimd", lambda h: h.dma_start(dbg["d_x1"][:, :], x1sc[:, :]), "dG")

        mm_mean2 = cnt["sT"]

        if debug:
            wait("vector", *all_lanes("dG"))
        xlo2 = []
        for m in range(DC):
            sl = m % 2
            if m >= 2:
                wait("gpsimd", ("sV", xlo2[m - 2]))
            wait("gpsimd", x1_tile_dG[m], ("sV", xlos[-1]),
                 ("sS", casts[-1]))
            d = run("gpsimd", lambda h, m=m, sl=sl: h.dma_start(
                xbig[:, sl * S:(sl + 1) * S], x1sc[m * 128:(m + 1) * 128, :]),
                "dG")
            wait("vector", d, ("sT", fus_mm_last))
            v = run("vector", lambda h, m=m, sl=sl: h.tensor_tensor(
                til(Bb, m), xbig[:, sl * S:(sl + 1) * S], til(A, m), OP.subtract),
                "sV")
            xlo2.append(v)

        wait("vector", ("sT", mm_mean2), ("sT", fus_mm_last))
        tmp2 = []
        for m in range(DC):
            wait("vector", ("sV", xlo2[m]))
            for qb in range(NB):
                run("vector", lambda h, m=m, qb=qb: h.tensor_tensor(
                    til(Cb, m, qb * 512, qb * 512 + 512),
                    til(Bb, m, qb * 512, qb * 512 + 512), psq23(qb),
                    OP.subtract), "sV")
            tmp2.append(cnt["sV"])
        c2_done = []
        for m in range(DC):
            wait("vector", ("sV", tmp2[m]))
            vc = run("vector", lambda h, m=m: h.tensor_tensor(
                til(Bb, m), til(A, m), til(Cb, m), OP.add), "sV")
            c2_done.append(vc)
        sq2 = []
        for m in range(DC):
            wait("scalar", ("sV", c2_done[m]))
            sc = run("scalar", lambda h, m=m: h.activation(
                til(Cb, m), til(Bb, m), AF.Square), "sS")
            sq2.append(sc)
        wait("tensor", ("sV", tmp2[-1]))
        for qb in range(NB):
            for m in range(DC):
                wait("tensor", ("sS", sq2[m]))
                run("tensor", lambda h, qb=qb, m=m: h.matmul(
                    psM(qb), ones_sc[:], til(Cb, m, qb * 512, qb * 512 + 512),
                    start=(m == 0), stop=(m == DC - 1)),
                    "sT" if m == DC - 1 else None)
        mm_var2 = cnt["sT"]
        wait("scalar", ("sT", mm_var2))
        for qb in range(NB):
            run("scalar", lambda h, qb=qb: h.activation(
                xbig[:, qb * 512:qb * 512 + 512], psM(qb), AF.Ln,
                bias=f32(1e-5)), "sS")
        s_ln2 = cnt["sS"]
        wait("scalar", ("sS", s_ln2))
        for qb in range(NB):
            run("scalar", lambda h, qb=qb: h.activation(
                xbig[:, S + qb * 512:S + qb * 512 + 512],
                xbig[:, qb * 512:qb * 512 + 512], AF.Exp,
                scale=f32(-0.5)), "sS")
        s_sq2 = cnt["sS"]
        wait("vector", ("sS", s_sq2))
        for m in range(DC):
            run("vector", lambda h, m=m: h.tensor_tensor(
                til(A, m), til(Bb, m), rstdB, OP.mult), "sV")
        ln2_ready = cnt["sV"]

        if debug:
            wait("gpsimd", ("sV", ln2_ready))
            for m in range(DC):
                run("gpsimd", lambda h, m=m: h.dma_start(
                    dbg["d_ln2"][m * 128:(m + 1) * 128, :], til(A, m)), "dG", 16)

        def ffn_stage(wd, src, dst, src_ready, dst_war):
            ev = 0

            def st_load(mb):
                wslot = mb % 2
                cw = 0
                for half in (0, 1):
                    cw = load_cols(wd, D, mb * 128, 128, 2 * wslot + half,
                                   wslot * 1024 + half * 512,
                                   nchunk_lo=half * 4, nchunk_hi=half * 4 + 4)
                return cw
            cw = st_load(0)
            for mb in range(DC):
                wslot = mb % 2
                nx = st_load(mb + 1) if mb + 1 < DC else None
                wait("tensor", cw, ("sV", src_ready), ("sV", dst_war))
                for qc in range(NB):
                    pw = ps[(mb * NB + qc) % 2][:, 0:512]
                    wait("tensor", ("sV", ev))
                    for c in range(DC):
                        run("tensor", lambda h, c=c, qc=qc, pw=pw, wslot=wslot:
                            h.matmul(
                            pw, wbf[:, wslot * 1024 + c * 128:
                                    wslot * 1024 + c * 128 + 128],
                            til(src, c, qc * 512, qc * 512 + 512),
                            start=(c == 0), stop=(c == DC - 1)),
                            "sT" if c == DC - 1 else None)
                    mmw = cnt["sT"]
                    mark_wbf(wslot * 1024, wslot * 1024 + 1024, mmw)
                    wait("vector", ("sT", mmw))
                    ev = run("vector", lambda h, mb=mb, qc=qc, pw=pw: h.tensor_copy(
                        til(dst, mb, qc * 512, qc * 512 + 512), pw), "sV")
                cw = nx
            return ev

        if debug:
            wait("vector", *all_lanes("dG"))

        sT_snap = cnt["sT"]
        sS_snap = cnt["sS"]
        def _slots(pool, need):
            out = []
            for t, w in pool:
                for o in range(0, w - 511, 512):
                    out.append((t, o))
                    if len(out) == need:
                        return out
            raise RuntimeError("not enough ffn staging slots")
        wt_slots = _slots([(e0, 2048), (me0, 2048), (qg, S), (kg, S),
                           (maskt, 1024), (segrep, S)], 2 * DC)
        a2_pool = [(vg, TC * VW), (outraw, 1024), (smallb, 3 * S + 64)]
        if DstX is not None:
            a2_pool.append((DstX, (DC - 2 * nDst) * 512))
        a2_slots = _slots(a2_pool, 2 * DC)
        assert not {(id(t), o) for t, o in wt_slots} & {(id(t), o) for t, o in a2_slots}

        def wt_ap(k, cc):
            t, o = wt_slots[2 * k + cc]
            return t[:, o:o + 512]

        def a2_ap(k, d2):
            t, o = a2_slots[2 * k + d2]
            return t[:, o:o + 512]

        rrslot = [0]

        def load_rows(dram, k, ap_fn):
            vs = []
            for hhf in range(2):
                sl2 = rrslot[0] % 4
                rrslot[0] += 1
                ws = wstate["wst"][sl2]
                wait("sync", ws if isinstance(ws, tuple) else None)
                d = run("sync", lambda h, k=k, hhf=hhf, sl2=sl2, dram=dram:
                        h.dma_start(
                    wst[:, sl2 * 512:sl2 * 512 + 512],
                    dram[k * 128:(k + 1) * 128, hhf * 512:hhf * 512 + 512]),
                    "dS")
                wait("scalar", d, ("sT", sT_snap),
                     ("sS", sS_snap), ("sV", st_done))
                v = run("scalar", lambda h, sl2=sl2, ap=ap_fn(k, hhf):
                        h.activation(
                    ap, wst[:, sl2 * 512:sl2 * 512 + 512], AF.Copy), "sS")
                wstate["wst"][sl2] = ("sS", v)
                vs.append(("sS", v))
            return vs[-1]

        wt_cast = [load_rows(w1d, k, wt_ap) for k in range(DC)]
        a2_cast = [load_rows(w3d, k, a2_ap) for k in range(DC)]

        def a1_load(a):
            wslot = a % 2
            cw = 0
            for half in (0, 1):
                cw = load_cols(w2d, D, a * 128, 128, 2 * wslot + half,
                               wslot * 1024 + half * 512, eng="scalar",
                               nchunk_lo=half * 4, nchunk_hi=half * 4 + 4)
            return cw
        ca1 = a1_load(0)
        ev = 0
        evp = {0: 0, 1: 0}
        for a in range(DC):
            wslot = a % 2
            nx1 = a1_load(a + 1) if a + 1 < DC else None
            wait("tensor", ca1, ("sV", ln2_ready), *wt_cast)
            for cc in range(2):
                par = (a * 2 + cc) % 2
                pw = ps[par][:, 0:512]
                wait("tensor", ("sV", evp[par]))
                for k in range(DC):
                    run("tensor", lambda h, a=a, cc=cc, k=k, pw=pw, wslot=wslot:
                        h.matmul(
                        pw, wbf[:, wslot * 1024 + k * 128:
                                wslot * 1024 + k * 128 + 128],
                        wt_ap(k, cc),
                        start=(k == 0), stop=(k == DC - 1)),
                        "sT" if k == DC - 1 else None)
                mmw = cnt["sT"]
                mark_wbf(wslot * 1024, wslot * 1024 + 1024, mmw)
                wait("vector", ("sT", mmw))
                ev = run("vector", lambda h, a=a, cc=cc, pw=pw: h.tensor_copy(
                    Bb[:, a * S + cc * 512:a * S + cc * 512 + 512], pw), "sV")
                evp[par] = ev
            ca1 = nx1
        m1_done = ev

        for cb in range(DC):
            for d2 in range(2):
                par = (cb * 2 + d2) % 2
                pw = ps[par][:, 0:512]
                wait("tensor", ("sV", evp[par]), ("sV", m1_done), *a2_cast)
                for a in range(DC):
                    run("tensor", lambda h, cb=cb, d2=d2, a=a, pw=pw: h.matmul(
                        pw, Bb[:, a * S + cb * 128:a * S + cb * 128 + 128],
                        a2_ap(a, d2),
                        start=(a == 0), stop=(a == DC - 1)),
                        "sT" if a == DC - 1 else None)
                mmw = cnt["sT"]
                wait("vector", ("sT", mmw))
                ev = run("vector", lambda h, cb=cb, d2=d2, pw=pw: h.tensor_copy(
                    wt_ap(cb, d2), pw), "sV")
                evp[par] = ev
        m2_done = ev

        NT3 = DC * NB
        NSL = min(8, (2 * S) // 512)
        s3pf = [None] * NT3
        s3_vo = [0] * NT3

        def s3_issue(t):
            mb, qc = divmod(t, NB)
            slot = t % NSL
            wait("gpsimd",
                 ("sV", s3_vo[t - NSL]) if t >= NSL else None,
                 x1_tile_dG[mb],
                 ("sV", ln2_ready) if t < NSL else None,
                 ("sS", s_sq2) if t < NSL else None)
            s3pf[t] = run("gpsimd", lambda h, mb=mb, qc=qc, slot=slot:
                          h.dma_start(
                xbig[:, slot * 512:slot * 512 + 512],
                x1sc[mb * 128:(mb + 1) * 128, qc * 512:qc * 512 + 512]), "dG")

        for t in range(min(NSL, NT3)):
            s3_issue(t)
        tslot_user = [0, 0]
        tanh_prev = [0, 0]
        oslot_user = [None, None]
        out_dG_last = 0
        for mb in range(DC):
            wait("tensor", ("sV", m2_done))
            for qc in range(NB):
                t = mb * NB + qc
                j = t % 2
                slot = t % NSL
                pw = ps[j][:, 0:512]
                wait("tensor", ("sS", tanh_prev[j]))
                for c in range(DC):
                    m2sl = wt_ap(c, mb // 4)
                    m2st = bass.AP(m2sl.tensor, m2sl.offset + (mb % 4) * 128,
                                   [m2sl.ap[0], [1, 128]])
                    run("tensor", lambda h, c=c, qc=qc, pw=pw, m2st=m2st:
                        h.matmul(
                        pw, m2st,
                        til(A, c, qc * 512, qc * 512 + 512),
                        start=(c == 0), stop=(c == DC - 1)),
                        "sT" if c == DC - 1 else None)
                mmw = cnt["sT"]
                wait("scalar", ("sT", mmw), ("sV", tslot_user[j]))
                s_t = run("scalar", lambda h, mb=mb, j=j, pw=pw: h.activation(
                    a2_ap(0, j), pw, AF.Tanh,
                    bias=bias_sb[:, 72 + mb:72 + mb + 1]), "sS")
                tanh_prev[j] = s_t
                wait("vector", ("sS", s_t), s3pf[t], oslot_user[j],
                     ("sS", x1slot_user[j][0]) if t < 2 else None,
                     x1slot_user[j][1] if t < 2 else None)
                vo = run("vector", lambda h, j=j, slot=slot: h.tensor_tensor(
                    xst[:, (2 + j) * 512:(2 + j) * 512 + 512],
                    xbig[:, slot * 512:slot * 512 + 512],
                    a2_ap(0, j), OP.add), "sV")
                tslot_user[j] = vo
                s3_vo[t] = vo
                wait("gpsimd", ("sV", vo))
                do = run("gpsimd", lambda h, mb=mb, qc=qc, j=j: h.dma_start(
                    outT[mb * 128:(mb + 1) * 128, qc * 512:qc * 512 + 512],
                    xst[:, (2 + j) * 512:(2 + j) * 512 + 512]), "dG")
                oslot_user[j] = do
                out_dG_last = do
                if t + NSL < NT3:
                    s3_issue(t + NSL)

        wait("gpsimd", *all_lanes("dG"))
        wait("sync", *all_lanes("dS"))

        block = E(nc.Block())

        @block.sync
        def _(sync):
            for fn in Q["sync"]:
                fn(sync)

        @block.gpsimd
        def _(gpsimd):
            for fn in Q["gpsimd"]:
                fn(gpsimd)

        @block.scalar
        def _(scalar):
            for fn in Q["scalar"]:
                fn(scalar)

        @block.vector
        def _(vector):
            for fn in Q["vector"]:
                fn(vector)

        @block.tensor
        def _(tensor):
            for fn in Q["tensor"]:
                fn(tensor)

    return nc, dbg


def _prep_inputs(inputs, S=2048, H=16):
    NSEG = 8
    """Pure layout: slice/transpose/pack the reference inputs per core."""
    g = lambda k: np.ascontiguousarray(np.asarray(inputs[k], dtype=np.float32))
    x = g("L0_attn_x")
    B = x.shape[0]
    FFh = H * DH

    def qk_pack(w_in):
        wq = w_in[0:FFh]
        wk = w_in[FFh:2 * FFh]
        out = np.empty((D, 2 * FFh), np.float32)
        for p in range(H // 2):
            out[:, p * 256:p * 256 + 128] = wq[p * 128:(p + 1) * 128].T
            out[:, p * 256 + 128:p * 256 + 256] = wk[p * 128:(p + 1) * 128].T
        return np.ascontiguousarray(out)

    lt_in = g("L0_attn_lt_w_in")
    st_in = g("L0_attn_st_w_in")
    shared = {
        "wqk_lt": qk_pack(lt_in),
        "wqk_st": qk_pack(st_in),
        "wv_lt": np.ascontiguousarray(lt_in[2 * FFh:3 * FFh].T),
        "wv_st": np.ascontiguousarray(st_in[2 * FFh:3 * FFh].T),
        "wout_lt": np.ascontiguousarray(g("L0_attn_lt_w_out").T),
        "wout_st": np.ascontiguousarray(g("L0_attn_st_w_out").T),
        "fuswT": np.ascontiguousarray(g("L0_attn_fus_w").T),
        "w1": np.ascontiguousarray(g("L0_ffn_weight").T),
        "w2": g("L0_ffn_aux_weight1"),
        "w3": g("L0_ffn_aux_weight2"),
        "protosT": np.ascontiguousarray(
            g("L0_attn_protos").T.reshape(8, 128, NSEG).transpose(1, 0, 2)
            .reshape(128, 8 * NSEG)),
        "biases": np.ascontiguousarray(np.concatenate([
            v.reshape(-1, 128).T if v.size == D else
            np.pad(v.reshape(-1, 128).T, ((0, 0), (0, 8 - v.size // 128)))
            for v in [
                g("L0_attn_lt_b_in")[0:FFh], g("L0_attn_lt_b_in")[FFh:2 * FFh],
                g("L0_attn_lt_b_in")[2 * FFh:3 * FFh], g("L0_attn_lt_b_out"),
                g("L0_attn_st_b_in")[0:FFh], g("L0_attn_st_b_in")[FFh:2 * FFh],
                g("L0_attn_st_b_in")[2 * FFh:3 * FFh], g("L0_attn_st_b_out"),
                g("L0_attn_fus_b"), g("L0_ffn_bias")]], axis=1)),
    }
    in_maps = []
    for b in range(B):
        m = dict(shared)
        m["xT"] = np.ascontiguousarray(x[b].T)
        in_maps.append(m)
    return in_maps


def kernel(**inputs):
    S, H = 2048, 16
    in_maps = _prep_inputs(inputs, S, H)
    nc, _ = build_core_program(S=S, H=H, debug=False)
    res = run_bass_kernel_spmd(nc, in_maps, core_ids=list(range(len(in_maps))))
    out = np.stack([np.asarray(r["outT"]).T for r in res.results], axis=0)
    return np.ascontiguousarray(out.astype(np.float32))


# version 32 (final)
# speedup vs baseline: 1.0357x; 1.0357x over previous
import contextlib
import numpy as np
import concourse.bass as bass
import concourse.mybir as mybir
from concourse.bass_utils import run_bass_kernel_spmd

F32 = mybir.dt.float32
BF16 = mybir.dt.bfloat16
U32 = mybir.dt.uint32
AF = mybir.ActivationFunctionType
OP = mybir.AluOpType
AX = mybir.AxisListType

ENGINES = ("sync", "gpsimd", "scalar", "vector", "tensor")
D = 1024
DC = 8
NSEG = 8
DH = 64


def f32(x):
    return np.float32(x)


def build_core_program(S=2048, H=16, debug=False):
    TC = S // 128
    NB = S // 512
    QHW = min(S, 1024)
    NQH = S // QHW
    NQB = QHW // 512
    NPAIR = H // 2
    HC = DH * H // 128
    FF = 2 * DH * H
    VHH = min(8, H)
    NV = H // VHH
    VW = (VHH // 2) * 161

    nc = bass.Bass("TRN2", target_bir_lowering=False)

    xT = nc.dram_tensor("xT", [D, S], F32, kind="ExternalInput").ap()
    wqk_lt = nc.dram_tensor("wqk_lt", [D, FF], F32, kind="ExternalInput").ap()
    wqk_st = nc.dram_tensor("wqk_st", [D, FF], F32, kind="ExternalInput").ap()
    wv_lt = nc.dram_tensor("wv_lt", [D, DH * H], F32, kind="ExternalInput").ap()
    wv_st = nc.dram_tensor("wv_st", [D, DH * H], F32, kind="ExternalInput").ap()
    wout_lt = nc.dram_tensor("wout_lt", [DH * H, D], F32, kind="ExternalInput").ap()
    wout_st = nc.dram_tensor("wout_st", [DH * H, D], F32, kind="ExternalInput").ap()
    fuswT = nc.dram_tensor("fuswT", [2 * D, D], F32, kind="ExternalInput").ap()
    w1d = nc.dram_tensor("w1", [D, D], F32, kind="ExternalInput").ap()
    w2d = nc.dram_tensor("w2", [D, D], F32, kind="ExternalInput").ap()
    w3d = nc.dram_tensor("w3", [D, D], F32, kind="ExternalInput").ap()
    protosT = nc.dram_tensor("protosT", [128, DC * NSEG], F32, kind="ExternalInput").ap()
    biases = nc.dram_tensor("biases", [128, 80], F32, kind="ExternalInput").ap()
    x1sc = nc.dram_tensor("x1sc", [D, S], F32, kind="Internal").ap()
    outT = nc.dram_tensor("outT", [D, S], F32, kind="ExternalOutput").ap()
    dbg = {}
    if debug:
        for nm, shp in [("d_sum", [128, 1024]), ("d_rep", [128, 512]),
                        ("d_ln1", [D, S]), ("d_seg", [S]), ("d_is1", [S]),
                        ("d_long", [D, S]), ("d_sel", [D, S]),
                        ("d_qg", [128, S]), ("d_kg", [128, S]),
                        ("d_ba", [D, S]), ("d_bs", [D, S]),
                        ("d_mask", [128, 1024]), ("d_me", [128, 1024]),
                        ("d_e0", [128, 1024]),
                        ("d_vq", [128, S]), ("d_vk", [128, S]),
                        ("d_ln2", [D, S]), ("d_x1", [D, S])]:
            dbg[nm] = nc.dram_tensor(nm, shp, F32, kind="ExternalOutput").ap()

    DSL = [f"dS{i}" for i in range(6)]
    DGL = [f"dG{i}" for i in range(4)]
    cnt = {s: 0 for s in DSL + DGL + ["sT", "sV", "sS", "sG"]}
    rrix = {"dS": 0, "dG": 0}
    hiw = {(e, s): 0 for e in ENGINES for s in cnt}
    Q = {e: [] for e in ENGINES}

    ctx = contextlib.ExitStack()
    with ctx as stack:
        E = stack.enter_context
        SEM = {s: E(nc.semaphore(s)) for s in cnt}

        A = E(nc.sbuf_tensor("A", [128, DC * S], BF16))
        Bb = E(nc.sbuf_tensor("Bb", [128, DC * S], BF16))
        Cb = E(nc.sbuf_tensor("Cb", [128, DC * S], BF16))

        qg = E(nc.sbuf_tensor("qg", [128, S], BF16))
        kg = E(nc.sbuf_tensor("kg", [128, S], BF16))
        vg = E(nc.sbuf_tensor("vg", [128, TC * VW], BF16))
        e0 = E(nc.sbuf_tensor("e0", [128, 2048], BF16))
        me0 = E(nc.sbuf_tensor("me0", [128, 2048], BF16))
        maskt = E(nc.sbuf_tensor("maskt", [128, 2 * 512], BF16))
        segrep = E(nc.sbuf_tensor("segrep", [128, S], BF16))
        outraw = E(nc.sbuf_tensor("outraw", [128, 1024], BF16))
        is1B = E(nc.sbuf_tensor("is1B", [128, S], mybir.dt.uint8))
        wst = E(nc.sbuf_tensor("wst", [128, 2048], F32))
        wbf = E(nc.sbuf_tensor("wbf", [128, 4096], BF16))
        xst = E(nc.sbuf_tensor("xst", [128, 4 * 512], F32))
        xbig = E(nc.sbuf_tensor("xbig", [128, 2 * S], F32))

        smallf = E(nc.sbuf_tensor("smallf", [128, 64], F32))
        smallb = E(nc.sbuf_tensor("smallb", [128, 3 * S + 64], BF16))
        identf = E(nc.sbuf_tensor("identf", [128, 128], F32))
        iotaF8 = E(nc.sbuf_tensor("iotaF8", [128, 8], F32))
        iotaP = E(nc.sbuf_tensor("iotaP", [128, 1], F32))
        iotaF128 = E(nc.sbuf_tensor("iotaF128", [128, 128], F32))
        ones_bf = E(nc.sbuf_tensor("ones_bf", [128, 128], BF16))
        ones_sc = E(nc.sbuf_tensor("ones_sc", [128, 128], BF16))
        onesf = E(nc.sbuf_tensor("onesf", [128, 128], F32))
        bias_sb = E(nc.sbuf_tensor("bias_sb", [128, 80], F32))
        bvb = E(nc.sbuf_tensor("bvb", [128, 16], BF16))
        protf = E(nc.sbuf_tensor("protf", [128, DC * 8], F32))
        prot_hi = E(nc.sbuf_tensor("prot_hi", [128, DC * 8], BF16))
        prot_lo = E(nc.sbuf_tensor("prot_lo", [128, DC * 8], BF16))
        ohN = E(nc.sbuf_tensor("ohN", [128, 8], F32))
        segu = E(nc.sbuf_tensor("segu", [128, 8], U32))
        segidN = E(nc.sbuf_tensor("segidN", [128, TC], F32))
        mx8 = E(nc.sbuf_tensor("mx8", [128, 1], F32))
        cntf = E(nc.sbuf_tensor("cntf", [8, 1], F32))
        cnt_rep = E(nc.sbuf_tensor("cnt_rep", [8, 128], BF16))
        iota8c = E(nc.sbuf_tensor("iota8c", [8, 1], BF16))

        wob = E(nc.sbuf_tensor("wob", [128, 16], F32))


        ps = [E(nc.psum_tensor(f"ps{i}", [128, 1024], F32)) for i in range(4)]
        rstdB = xbig[:, S:2 * S]
        recb2 = xst

        tanhs = e0
        nDst = S // 512
        DstX = E(nc.sbuf_tensor("DstX", [128, max(0, (DC - 2 * nDst)) * 512],
                                BF16)) if DC > 2 * nDst else None

        def Dst_ap(mb):
            if mb < nDst:
                return qg[:, (mb % nDst) * 512:(mb % nDst) * 512 + 512]
            if mb < 2 * nDst:
                m2 = mb - nDst
                return kg[:, m2 * 512:m2 * 512 + 512]
            m3 = mb - 2 * nDst
            return DstX[:, m3 * 512:m3 * 512 + 512]

        def emit(eng, fn):
            Q[eng].append(fn)

        def wait(eng, *deps):
            todo = []
            for d in deps:
                if d is None:
                    continue
                s, v = d
                if v is None:
                    continue
                if v is None or v <= 0:
                    continue
                if hiw[(eng, s)] >= v:
                    continue
                hiw[(eng, s)] = v
                todo.append((s, v))
            if todo:
                def f(h, todo=tuple(todo)):
                    for s, v in todo:
                        h.wait_ge(SEM[s], v)
                emit(eng, f)

        def run(eng, make, inc=None, k=1):
            if inc is None:
                emit(eng, lambda h, m=make: m(h))
                return None
            if inc in ("dS", "dG"):
                lanes = DSL if inc == "dS" else DGL
                s = lanes[rrix[inc] % len(lanes)]
                rrix[inc] += 1
                wait(eng, (s, cnt[s]))
                cnt[s] += 16
                v = cnt[s]

                def fd(h, m=make, s=s):
                    m(h).then_inc(SEM[s], 16)
                emit(eng, fd)
                return (s, v)
            cnt[inc] += k
            v = cnt[inc]

            def f(h, m=make, s=inc, kk=k):
                m(h).then_inc(SEM[s], kk)
            emit(eng, f)
            return v

        def all_lanes(pool):
            lanes = DSL if pool == "dS" else DGL
            return [(s, cnt[s]) for s in lanes]

        def til(t, m, c0=0, c1=None):
            c1 = S if c1 is None else c1
            return t[:, m * S + c0: m * S + c1]

        def psM(qb):
            return ps[qb // 2][:, (qb % 2) * 512:(qb % 2) * 512 + 512]

        def psq23(qb):
            return ps[2 + qb // 2][:, (qb % 2) * 512:(qb % 2) * 512 + 512]

        wstate = {"wst": [None, None, None, None],
                  "wkey": [0] * 8}

        def mark_wbf(lo, hi, val):
            for kk in range(lo // 512, (hi + 511) // 512):
                if wstate["wkey"][kk] < val:
                    wstate["wkey"][kk] = val

        def load_cols(dram, CT, col0, Wd, wst_slot, wbf_off, nchunk_lo=0,
                      nchunk_hi=DC, eng="gpsimd"):
            nch = nchunk_hi - nchunk_lo
            assert nch * Wd <= 512
            so = wst_slot * 512
            ws = wstate["wst"][wst_slot]
            wait("sync", ws if isinstance(ws, tuple) else None)
            dv = run("sync", lambda h: h.dma_start(
                bass.AP(wst[:].tensor, so, [wst[:].ap[0], [Wd, nch], [1, Wd]]),
                bass.AP(dram.tensor, nchunk_lo * 128 * CT + col0,
                        [[CT, 128], [128 * CT, nch], [1, Wd]])), "dS")
            deps = [dv] + [("sT", wstate["wkey"][kk])
                           for kk in range(wbf_off // 512,
                                           (wbf_off + nch * Wd + 511) // 512)]
            wait(eng, *deps)
            if eng == "scalar":
                v = run(eng, lambda h: h.activation(
                    wbf[:, wbf_off:wbf_off + nch * Wd],
                    wst[:, so:so + nch * Wd], AF.Copy), "sS")
                wstate["wst"][wst_slot] = ("sS", v)
                return ("sS", v)
            v = run(eng, lambda h: h.tensor_copy(
                wbf[:, wbf_off:wbf_off + nch * Wd],
                wst[:, so:so + nch * Wd]), "sG")
            wstate["wst"][wst_slot] = ("sG", v)
            return ("sG", v)

        for mk in (lambda h: h.iota(iotaF8[:], [[1, 8]], channel_multiplier=0,
                                    allow_small_or_imprecise_dtypes=True),
                   lambda h: h.iota(iotaP[:], [[1, 1]], channel_multiplier=1,
                                    allow_small_or_imprecise_dtypes=True),
                   lambda h: h.iota(iotaF128[:], [[1, 128]], channel_multiplier=0,
                                    allow_small_or_imprecise_dtypes=True),
                   lambda h: h.memset(ones_bf[:], 1.0),
                   lambda h: h.memset(ones_sc[:], 1.0 / 1024.0),
                   lambda h: h.memset(onesf[:], 1.0)):
            run("gpsimd", mk, "sG")
        run("gpsimd", lambda h: h.memset(
            bass.AP(vg[:].tensor, 64,
                    [vg[:].ap[0], [VW, TC], [161, VHH // 2], [1, 2]]), 1.0), "sG")
        run("gpsimd", lambda h: h.memset(
            bass.AP(vg[:].tensor, 66,
                    [vg[:].ap[0], [VW, TC], [161, VHH // 2], [1, 31]]), 0.0), "sG")
        sG0 = cnt["sG"]

        d_bias = run("sync", lambda h: h.dma_start(
            bias_sb[:, 0:80], biases[:, :]), "dS")
        d_prot = run("sync", lambda h: h.dma_start(
            protf[:, 0:DC * 8], protosT[:, :]), "dS")

        wait("vector", ("sG", sG0), d_prot, d_bias)
        run("vector", lambda h: h.tensor_scalar(
            identf[:], iotaF128[:], iotaP[:, 0:1], None, OP.is_equal), "sV")
        run("vector", lambda h: h.tensor_copy(iota8c[:], iotaP[0:8, 0:1]), "sV")
        run("vector", lambda h: h.tensor_copy(bvb[:, 0:8], bias_sb[:, 16:24]), "sV")
        run("vector", lambda h: h.tensor_copy(bvb[:, 8:16], bias_sb[:, 48:56]), "sV")
        wait("scalar", d_prot)
        s_ph = run("scalar", lambda h: h.activation(prot_hi[:], protf[:], AF.Copy), "sS")
        wait("vector", ("sS", s_ph))
        v_pl = run("vector", lambda h: h.tensor_tensor(
            prot_lo[:], protf[:], prot_hi[:], OP.subtract), "sV")
        v_const = cnt["sV"]

        casts, xlos = [], []
        for m in range(DC):
            sl = m % 2
            if m >= 2:
                wait("sync", ("sS", casts[m - 2]), ("sV", xlos[m - 2]))
            d = run("sync", lambda h, m=m, sl=sl: h.dma_start(
                xbig[:, sl * S:(sl + 1) * S], xT[m * 128:(m + 1) * 128, :]), "dS")
            wait("scalar", d)
            c = run("scalar", lambda h, m=m, sl=sl: h.activation(
                til(A, m), xbig[:, sl * S:(sl + 1) * S], AF.Copy), "sS")
            casts.append(c)
            wait("vector", d, ("sS", c))
            v = run("vector", lambda h, m=m, sl=sl: h.tensor_tensor(
                til(Bb, m), xbig[:, sl * S:(sl + 1) * S], til(A, m), OP.subtract), "sV")
            xlos.append(v)

        wait("tensor", ("sG", sG0))
        for qb in range(NB):
            for m in range(DC):
                wait("tensor", ("sS", casts[m]))
                run("tensor", lambda h, qb=qb, m=m: h.matmul(
                    psM(qb), ones_sc[:], til(A, m, qb * 512, qb * 512 + 512),
                    start=(m == 0), stop=False if m < DC - 1 else True),
                    "sT" if m == DC - 1 else None)
        mm_mean = cnt["sT"]

        wait("scalar", ("sT", mm_mean))
        for qb in range(NB):
            run("scalar", lambda h, qb=qb: h.activation(
                smallb[0:1, S + qb * 512:S + qb * 512 + 512], psM(qb)[0:1, :],
                AF.Copy), "sS")
        s_mh = cnt["sS"]
        wait("vector", ("sS", s_mh))
        for qb in range(NB):
            run("vector", lambda h, qb=qb: h.tensor_tensor(
                smallb[0:1, 2 * S + qb * 512:2 * S + qb * 512 + 512],
                psM(qb)[0:1, :],
                smallb[0:1, S + qb * 512:S + qb * 512 + 512], OP.subtract), "sV")
        v_ml = cnt["sV"]

        sprop = ps[2][0:1, 512:520]
        wait("tensor", ("sV", v_pl))
        for i, pr in enumerate([prot_hi, prot_lo]):
            for m in range(DC):
                last = (i == 1 and m == DC - 1)
                run("tensor", lambda h, pr=pr, m=m, i=i, last=last: h.matmul(
                    sprop, ones_bf[:, 0:1], pr[:, m * 8:(m + 1) * 8],
                    start=(i == 0 and m == 0), stop=last),
                    "sT" if last else None)
        mm_sp = cnt["sT"]
        wait("vector", ("sT", mm_sp))
        run("vector", lambda h: h.tensor_scalar(
            smallb[0:1, 3 * S:3 * S + 8], sprop, -1.0, None, OP.mult), "sV")
        v_nf = run("vector", lambda h: h.tensor_scalar(
            smallf[0:1, 16:24], sprop, -1.0, None, OP.mult), "sV")
        wait("vector", ("sV", v_nf))
        v_nl = run("vector", lambda h: h.tensor_tensor(
            smallb[0:1, 3 * S + 8:3 * S + 16], smallf[0:1, 16:24],
            smallb[0:1, 3 * S:3 * S + 8], OP.subtract), "sV")

        tr_done, seg_done = [], []
        tr_mm = 0
        tmp_ops = [(m, qb) for m in range(DC) for qb in range(NB)]
        tmp_ch = -(-len(tmp_ops) // TC)
        tmp_done = {}
        for t in range(TC):
            pspi = 2 + (t % 2)
            psP = ps[pspi][:, 0:8]
            wait("tensor", ("sV", v_nl), ("sV", xlos[-1]),
                 ("sV", seg_done[t - 2] if t >= 2 else None))
            first = True
            for src, pr in ((A, prot_hi), (A, prot_lo), (Bb, prot_hi), (Bb, prot_lo)):
                for m in range(DC):
                    run("tensor", lambda h, src=src, pr=pr, m=m, t=t, psP=psP,
                        first=first: h.matmul(
                        psP, til(src, m, t * 128, t * 128 + 128),
                        pr[:, m * 8:(m + 1) * 8], start=first, stop=False), None)
                    first = False
            for i, (mo, nr) in enumerate(((S, 0), (S, 8), (2 * S, 0),
                                          (2 * S, 8))):
                run("tensor", lambda h, mo=mo, nr=nr, t=t, psP=psP, i=i: h.matmul(
                    psP, smallb[0:1, mo + t * 128:mo + t * 128 + 128],
                    smallb[0:1, 3 * S + nr:3 * S + nr + 8],
                    start=False, stop=(i == 3)),
                    "sT" if i == 3 else None)
            mm = cnt["sT"]
            wait("vector", ("sT", mm))
            v1 = run("vector", lambda h, psP=psP: h.tensor_reduce(
                mx8[:, 0:1], psP, AX.X, OP.max), "sV")
            wait("vector", ("sV", v1))
            v2 = run("vector", lambda h, psP=psP: h.max_index(
                segu[:], bass.AP(mx8[:].tensor, 0, [mx8[:].ap[0], [0, 8]]), psP), "sV")
            wait("vector", ("sV", v2))
            v3 = run("vector", lambda h, t=t: h.tensor_copy(
                segidN[:, t:t + 1], segu[:, 0:1]), "sV")
            wait("vector", ("sV", v3), ("sT", tr_mm if t >= 1 else None))
            v4 = run("vector", lambda h, t=t: h.tensor_scalar(
                ohN[:], iotaF8[:], segidN[:, t:t + 1], None, OP.is_equal), "sV")
            seg_done.append(v4)
            for m2, qb2 in tmp_ops[t * tmp_ch:(t + 1) * tmp_ch]:
                wait("vector", ("sT", mm_mean))
                run("vector", lambda h, m=m2, qb=qb2: h.tensor_tensor(
                    til(Cb, m, qb * 512, qb * 512 + 512),
                    til(Bb, m, qb * 512, qb * 512 + 512), psM(qb),
                    OP.subtract), "sV")
                tmp_done[m2] = cnt["sV"]
            psT_sl = ps[pspi][0:8, 640:768]
            wait("tensor", ("sV", v4), ("sV", v_const),
                 ("sV", tr_done[t - 2] if t >= 2 else None))
            tr_mm = run("tensor", lambda h, psT_sl=psT_sl: h.transpose(
                psT_sl, ohN[:], identf[:]), "sT")
            wait("vector", ("sT", tr_mm))
            v5 = run("vector", lambda h, t=t, psT_sl=psT_sl: h.tensor_copy(
                smallb[0:8, t * 128:(t + 1) * 128], psT_sl), "sV")
            tr_done.append(v5)
        mm_ps_last = cnt["sT"]

        wait("vector", ("sV", tr_done[-1]))
        v_cnt = run("vector", lambda h: h.tensor_reduce(
            cntf[:], smallb[0:8, 0:S], AX.X, OP.add), "sV")
        wait("vector", ("sV", v_cnt))
        v_cr = run("vector", lambda h: h.tensor_scalar(
            cnt_rep[:], ones_bf[0:8, 0:128], cntf[0:8, 0:1], None, OP.mult), "sV")
        wait("tensor", ("sV", v_cr))
        for qb in range(NB):
            run("tensor", lambda h, qb=qb: h.matmul(
                psq23(qb), cnt_rep[:], smallb[0:8, qb * 512:qb * 512 + 512],
                start=True, stop=True), "sT")
        mm_tc = cnt["sT"]
        wait("scalar", ("sT", mm_tc))
        for qb in range(NB):
            run("scalar", lambda h, qb=qb: h.activation(
                is1B[:, qb * 512:qb * 512 + 512], psq23(qb), AF.Relu,
                scale=f32(-1.0), bias=f32(2.0)), "sS")
        s_is1 = cnt["sS"]
        wait("tensor", ("sS", s_is1))
        for qb in range(NB):
            run("tensor", lambda h, qb=qb: h.matmul(
                psq23(qb)[0:1, :], iota8c[:], smallb[0:8, qb * 512:qb * 512 + 512],
                start=True, stop=True), "sT")
        mm_sg = cnt["sT"]
        wait("vector", ("sT", mm_sg))
        for qb in range(NB):
            run("vector", lambda h, qb=qb: h.tensor_copy(
                smallb[0:1, S + qb * 512:S + qb * 512 + 512], psq23(qb)[0:1, :]),
                "sV")
        v_sgT = cnt["sV"]
        wait("tensor", ("sV", v_sgT))
        for qb in range(NB):
            run("tensor", lambda h, qb=qb: h.matmul(
                psq23(qb), ones_bf[0:1, 0:128],
                smallb[0:1, S + qb * 512:S + qb * 512 + 512],
                start=True, stop=True), "sT")
        mm_sr = cnt["sT"]
        wait("vector", ("sT", mm_sr))
        for qb in range(NB):
            run("vector", lambda h, qb=qb: h.tensor_copy(
                segrep[:, qb * 512:qb * 512 + 512], psq23(qb)), "sV")
        seg_ready = cnt["sV"]

        cbf_done = []
        for m in range(DC):
            wait("vector", ("sT", mm_ps_last), ("sV", tmp_done[m]))
            vc = run("vector", lambda h, m=m: h.tensor_tensor(
                til(Bb, m), til(A, m), til(Cb, m), OP.add), "sV")
            cbf_done.append(vc)
        sq_done = []
        for m in range(DC):
            wait("scalar", ("sV", cbf_done[m]))
            sc = run("scalar", lambda h, m=m: h.activation(
                til(Cb, m), til(Bb, m), AF.Square), "sS")
            sq_done.append(sc)
        wait("tensor", ("sV", tmp_done[DC - 1]), ("sV", v_ml))
        for qb in range(NB):
            for m in range(DC):
                wait("tensor", ("sS", sq_done[m]))
                run("tensor", lambda h, qb=qb, m=m: h.matmul(
                    psM(qb), ones_sc[:], til(Cb, m, qb * 512, qb * 512 + 512),
                    start=(m == 0), stop=(m == DC - 1)),
                    "sT" if m == DC - 1 else None)
        mm_var = cnt["sT"]
        wait("scalar", ("sT", mm_var))
        for qb in range(NB):
            run("scalar", lambda h, qb=qb: h.activation(
                xbig[:, qb * 512:qb * 512 + 512], psM(qb), AF.Ln,
                bias=f32(1e-5)), "sS")
        s_ln = cnt["sS"]
        rstd_qb = []
        wait("scalar", ("sS", s_ln))
        for qb in range(NB):
            run("scalar", lambda h, qb=qb: h.activation(
                xbig[:, S + qb * 512:S + qb * 512 + 512],
                xbig[:, qb * 512:qb * 512 + 512], AF.Exp,
                scale=f32(-0.5)), "sS")
            rstd_qb.append(cnt["sS"])
        s_sq = cnt["sS"]
        ln_piece = [0] * NB
        for qb in range(NB):
            wait("vector", ("sS", rstd_qb[qb]))
            for m in range(DC):
                run("vector", lambda h, m=m, qb=qb: h.tensor_tensor(
                    til(A, m, qb * 512, qb * 512 + 512),
                    til(Bb, m, qb * 512, qb * 512 + 512),
                    rstdB[:, qb * 512:qb * 512 + 512], OP.mult), "sV")
            ln_piece[qb] = cnt["sV"]
        ln_ready = cnt["sV"]

        if debug:
            wait("gpsimd", ("sV", ln_ready), ("sS", s_is1), ("sV", seg_ready))
            for m in range(DC):
                run("gpsimd", lambda h, m=m: h.dma_start(
                    dbg["d_ln1"][m * 128:(m + 1) * 128, :], til(A, m)), "dG", 16)
            run("gpsimd", lambda h: h.dma_start(
                bass.AP(dbg["d_seg"].tensor, 0, [[1, 128], [128, TC]]),
                segidN[:, 0:TC], allow_slow_non_contiguous=True), "dG", 16)
            run("vector", lambda h: h.tensor_copy(
                e0[0:1, 0:S], is1B[0:1, 0:S]), "sV")
            wait("gpsimd", ("sV", cnt["sV"]))
            run("gpsimd", lambda h: h.dma_start(
                bass.AP(dbg["d_is1"].tensor, 0, [[1, S]]), e0[0:1, 0:S]), "dG")

        attn_state = {"ps01_rd": 0,
                      "norm_done": 0}

        def mha(wqk, wv, bq, bk, bo_col, bv_col, wob_col, wout, masked):
            pair_done_sV = 0
            last_select = 0
            for hv in range(NV):
                if attn_state.get("tsc") is not None:
                    attn_state.pop("tsc")()
                if attn_state.get("pt") is not None:
                    attn_state.pop("pt")()
                VW64 = VHH * 64
                use_qk_stage = hv == 0 and DC * VW64 <= 2 * S
                if use_qk_stage:
                    npb = S // VW64

                    def vst_ap(m):
                        buf = qg if m < npb else kg
                        return buf[:, (m % npb) * VW64:(m % npb) * VW64 + VW64]
                    dep_rd = ("sT", max(attn_state.get("mms_sl", [0, 0])))
                    dep_wr = ("sV", attn_state.get("qkev", 0))
                    vw = 0
                    for c in range(DC):
                        so = (c % 4) * 512
                        ws = wstate["wst"][c % 4]
                        wait("sync", ws if isinstance(ws, tuple) else None)
                        dv = run("sync", lambda h, c=c, so=so, hv=hv:
                                 h.dma_start(
                            bass.AP(wst[:].tensor, so,
                                    [wst[:].ap[0], [VW64, 1], [1, VW64]]),
                            bass.AP(wv.tensor,
                                    c * 128 * (DH * H) + hv * VW64,
                                    [[DH * H, 128], [128 * (DH * H), 1],
                                     [1, VW64]])), "dS")
                        wait("gpsimd", dv, dep_rd, dep_wr)
                        v = run("gpsimd", lambda h, ap=vst_ap(c), so=so:
                                h.tensor_copy(
                            ap, wst[:, so:so + VW64]), "sG")
                        wstate["wst"][c % 4] = ("sG", v)
                        vw = ("sG", v)
                    if debug:
                        wait("gpsimd", vw)
                        run("gpsimd", lambda h: h.dma_start(
                            dbg["d_vq"][:, :], qg[:, 0:S]), "dG")
                        vw = ("sG", cnt["sG"])
                        run("gpsimd", lambda h: h.dma_start(
                            dbg["d_vk"][:, :], kg[:, 0:S]), "dG")
                else:
                    def vst_ap(m):
                        return wbf[:, m * 512:m * 512 + VW64]
                    vw = 0
                    for c in range(DC):
                        vw = load_cols(wv, DH * H, hv * VW64, VW64,
                                       c % 4, c * 512, nchunk_lo=c,
                                       nchunk_hi=c + 1)
                vev = []
                wait("tensor", vw,
                     ("sV", attn_state["norm_done"]),
                     attn_state.get("or_done"))
                for t in range(TC):
                    pp = ps[2 + (t % 2)][:, 0:VHH * 64]
                    wait("tensor", ("sV", vev[t - 2] if t >= 2 else None),
                         ("sV", ln_piece[min(t // 4, NB - 1)]))
                    for m in range(DC):
                        run("tensor", lambda h, t=t, m=m, pp=pp,
                            vap=vst_ap(m): h.matmul(
                            pp, til(A, m, t * 128, t * 128 + 128),
                            vap,
                            start=(m == 0), stop=(m == DC - 1)),
                            "sT" if m == DC - 1 else None)
                    mmv = cnt["sT"]
                    if not use_qk_stage:
                        mark_wbf(0, 4096, mmv)
                    wait("vector", ("sT", mmv))
                    run("vector", lambda h, t=t, pp=pp: h.tensor_copy(
                        bass.AP(vg[:].tensor, t * VW,
                                [vg[:].ap[0], [161, VHH // 2], [1, 64]]),
                        bass.AP(pp.tensor, pp.offset,
                                [pp.ap[0], [128, VHH // 2], [1, 64]])), "sV")
                    ve = run("vector", lambda h, t=t, pp=pp: h.tensor_copy(
                        bass.AP(vg[:].tensor, t * VW + 97,
                                [vg[:].ap[0], [161, VHH // 2], [1, 64]]),
                        bass.AP(pp.tensor, pp.offset + 64,
                                [pp.ap[0], [128, VHH // 2], [1, 64]])), "sV")
                    vev.append(ve)
                v_ready = vev[-1]

                for p in range(hv * VHH // 2, (hv + 1) * VHH // 2):
                    wslot = (p + 1) % 2 if masked else p % 2
                    staged = attn_state.get("nx_p") == p
                    kstaged = staged and attn_state.get("nx_kinj")
                    kbuf = qg if (p % 2 == 0 and p >= 2) else kg
                    if staged:
                        attn_state.pop("nx_p")
                        attn_state.pop("nx_cq")
                        attn_state.pop("nx_kinj")
                        qgev = attn_state.pop("stq_ev")
                        stk_ev = attn_state.pop("stk_ev")
                        stqbuf = attn_state.pop("nx_buf")
                        ck = attn_state.pop("nx_ck")
                        attn_state.pop("nx_wsl")
                        if ck is None:
                            for half in (0, 1):
                                ck = load_cols(
                                    wqk, FF, p * 256 + 128, 128,
                                    2 * wslot + 1,
                                    (2 * wslot + 1) * 1024 + half * 512,
                                    nchunk_lo=half * 4,
                                    nchunk_hi=half * 4 + 4)

                        def qap(r0, r1, c0, w, stqbuf=stqbuf, p=p):
                            return stqbuf[r0:r1, p * S + c0:p * S + c0 + w]
                    else:
                        cq = ck = 0
                        for j in (0, 1):
                            for half in (0, 1):
                                c = load_cols(wqk, FF, p * 256 + j * 128, 128,
                                              2 * wslot + j,
                                              (2 * wslot + j) * 1024 + half * 512,
                                              nchunk_lo=half * 4,
                                              nchunk_hi=half * 4 + 4)
                            if j == 0:
                                cq = c
                            else:
                                ck = c

                        def qap(r0, r1, c0, w):
                            return qg[r0:r1, c0:c0 + w]
                    if p + 1 < H // 2:
                        pn = p + 1
                        kinj = pn >= 2
                        same_hv = pn < (hv + 1) * VHH // 2
                        wsl_n = (pn + 1) % 2 if masked else pn % 2
                        cq2 = 0
                        ck2 = None
                        for j in ((0, 1) if (same_hv or kinj) else (0,)):
                            for half in (0, 1):
                                c = load_cols(wqk, FF, pn * 256 + j * 128, 128,
                                              2 * wsl_n + j,
                                              (2 * wsl_n + j) * 1024 + half * 512,
                                              nchunk_lo=half * 4,
                                              nchunk_hi=half * 4 + 4)
                            if j == 0:
                                cq2 = c
                            else:
                                ck2 = c
                        attn_state["nx_p"] = pn
                        attn_state["nx_cq"] = cq2
                        attn_state["nx_ck"] = ck2
                        attn_state["nx_kinj"] = kinj
                        attn_state["nx_wsl"] = wsl_n
                        attn_state["nx_buf"] = Bb if masked else Cb
                        attn_state["stq_ev"] = [0] * NB
                        attn_state["stk_ev"] = [0] * NB

                    def stage_q_part(j, pc, bq=bq):
                        par1 = 1 - (j % 2)
                        pst = ps[2 + par1]
                        wsl_n = attn_state["nx_wsl"]
                        pn = attn_state["nx_p"]
                        if pc == 0:
                            wait("tensor", attn_state["nx_cq"],
                                 ("sV", attn_state.get("avp%d" % par1, 0)))
                        for m in (2 * pc, 2 * pc + 1):
                            run("tensor", lambda h, m=m, j=j, pst=pst,
                                wsl_n=wsl_n: h.matmul(
                                pst[:, 0:512],
                                wbf[:, 2 * wsl_n * 1024 + m * 128:
                                    2 * wsl_n * 1024 + m * 128 + 128],
                                til(A, m, j * 512, j * 512 + 512),
                                start=(m == 0), stop=(m == DC - 1)),
                                "sT" if m == DC - 1 else None)
                        if pc < DC // 2 - 1:
                            return
                        g = cnt["sT"]
                        mark_wbf(2 * wsl_n * 1024, 2 * wsl_n * 1024 + 1024, g)
                        buf = attn_state["nx_buf"]
                        wait("vector", ("sT", g))
                        ev = run("vector", lambda h, j=j, pst=pst, buf=buf,
                                 pn=pn, bq=bq: h.tensor_scalar(
                            buf[:, pn * S + j * 512:pn * S + j * 512 + 512],
                            pst[:, 0:512],
                            bias_sb[:, bq * 8 + pn:bq * 8 + pn + 1], 0.125,
                            OP.add, OP.mult), "sV")
                        attn_state["stq_ev"][j] = ev
                        attn_state["avp%d" % par1] = ev

                    def stage_k_part(j, pc, bk=bk):
                        par1 = 1 - (j % 2)
                        pst = ps[2 + par1]
                        wsl_n = attn_state["nx_wsl"]
                        pn = attn_state["nx_p"]
                        if pc == 0:
                            wait("tensor", attn_state["nx_ck"],
                                 ("sV", attn_state.get("avp%d" % par1, 0)))
                        for m in (2 * pc, 2 * pc + 1):
                            run("tensor", lambda h, m=m, j=j, pst=pst,
                                wsl_n=wsl_n: h.matmul(
                                pst[:, 512:1024],
                                wbf[:, (2 * wsl_n + 1) * 1024 + m * 128:
                                    (2 * wsl_n + 1) * 1024 + m * 128 + 128],
                                til(A, m, j * 512, j * 512 + 512),
                                start=(m == 0), stop=(m == DC - 1)),
                                "sT" if m == DC - 1 else None)
                        if pc < DC // 2 - 1:
                            return
                        g = cnt["sT"]
                        mark_wbf((2 * wsl_n + 1) * 1024,
                                 (2 * wsl_n + 1) * 1024 + 1024, g)
                        pn_kbuf = qg if pn % 2 == 0 else kg
                        wait("vector", ("sT", g))
                        ev = run("vector", lambda h, j=j, pst=pst,
                                 kb=pn_kbuf, pn=pn, bk=bk: h.tensor_scalar(
                            kb[:, j * 512:j * 512 + 512],
                            pst[:, 512:1024],
                            bias_sb[:, bk * 8 + pn:bk * 8 + pn + 1], None,
                            OP.add), "sV")
                        attn_state["stk_ev"][j] = ev
                        attn_state["avp%d" % par1] = ev

                    if attn_state.get("tsc") is not None:
                        attn_state.pop("tsc")()
                    if attn_state.get("pt") is not None:
                        attn_state.pop("pt")()
                    if kstaged:
                        kgev = stk_ev
                    elif staged:
                        wait("tensor", ck, ("sV", v_ready),
                             ("sV", attn_state["norm_done"]),
                             attn_state.get("or_done"),
                             ("sV", attn_state.get("avp0", 0)),
                             ("sS", st_exp[0]))
                    else:
                        wait("tensor", ck, cq, ("sV", v_ready),
                             ("sV", attn_state["norm_done"]),
                             attn_state.get("or_done"))
                    if not kstaged:
                        kgev = []
                    if not staged:
                        qgev = []
                    for qb in range(NB if not kstaged else 0):
                        if staged:
                            psk = (ps[2][:, (qb % 2) * 512:(qb % 2) * 512 + 512]
                                   if qb < 2 else
                                   ps[0][:, (qb % 2) * 512:(qb % 2) * 512 + 512])
                        else:
                            psq = ps[qb // 2][:, (qb % 2) * 512:
                                              (qb % 2) * 512 + 512]
                            psk = psq23(qb)
                            for m in range(DC):
                                run("tensor", lambda h, qb=qb, m=m, psq=psq,
                                    wslot=wslot: h.matmul(
                                    psq, wbf[:, 2 * wslot * 1024 + m * 128:
                                             2 * wslot * 1024 + m * 128 + 128],
                                    til(A, m, qb * 512, qb * 512 + 512),
                                    start=(m == 0), stop=(m == DC - 1)),
                                    "sT" if m == DC - 1 else None)
                        for m in range(DC):
                            run("tensor", lambda h, qb=qb, m=m, psk=psk, wslot=wslot:
                                h.matmul(
                                psk, wbf[:, (2 * wslot + 1) * 1024 + m * 128:
                                         (2 * wslot + 1) * 1024 + m * 128 + 128],
                                til(A, m, qb * 512, qb * 512 + 512),
                                start=(m == 0), stop=(m == DC - 1)),
                                "sT" if m == DC - 1 else None)
                        gk = cnt["sT"]
                        wait("vector", ("sT", gk))
                        if debug and qb == 0:
                            wait("vector", *all_lanes("dG"))
                        run("vector", lambda h, qb=qb, psk=psk, p=p,
                            kbuf=kbuf: h.tensor_scalar(
                            kbuf[:, qb * 512:qb * 512 + 512], psk,
                            bias_sb[:, bk * 8 + p:bk * 8 + p + 1], None,
                            OP.add), "sV")
                        kgev.append(cnt["sV"])
                        if not staged:
                            run("vector", lambda h, qb=qb, psq=psq, p=p:
                                h.tensor_scalar(
                                qg[:, qb * 512:qb * 512 + 512], psq,
                                bias_sb[:, bq * 8 + p:bq * 8 + p + 1], 0.125,
                                OP.add, OP.mult), "sV")
                            qgev.append(cnt["sV"])
                    mmqk = cnt["sT"]
                    mark_wbf((2 * wslot + 1) * 1024,
                             (2 * wslot + 1) * 1024 + 1024, mmqk)
                    if not staged:
                        mark_wbf(2 * wslot * 1024, 2 * wslot * 1024 + 1024,
                                 mmqk)
                    qk_ready = cnt["sV"]
                    attn_state["qkev"] = qk_ready
                    if debug and p == 0 and not masked:
                        wait("gpsimd", ("sV", qk_ready))
                        run("gpsimd", lambda h: h.dma_start(
                            dbg["d_qg"][:, :], qg[:, 0:S]), "dG")
                        run("gpsimd", lambda h: h.dma_start(
                            dbg["d_kg"][:, :], kg[:, 0:S]), "dG")

                    stage_kcs = ([4, 7, 10, 13] if TC >= 14
                                 else [3, 4, 5, 6])[:DC // 2]
                    stagek_kcs = ([5, 8, 11, 14] if TC >= 15
                                  else [3, 4, 5, 6])[:DC // 2]
                    hl0 = (2 * p) % VHH
                    jr = hl0 // 2
                    st_exp = attn_state.setdefault("exp_sl", [0, 0])
                    st_av = attn_state.setdefault("av_sl", [0, 0])
                    st_mult = attn_state.setdefault("mult_sl", [0, 0])
                    st_mms = attn_state.setdefault("mms_sl", [0, 0])
                    st_mgen = attn_state.setdefault("mgen_sl", [0, 0])
                    st_exp4 = attn_state.setdefault("exp4_sl", [0, 0, 0, 0])
                    st_av4 = attn_state.setdefault("av4_sl", [0, 0, 0, 0])

                    def arena(kc):
                        s4 = kc % 4
                        buf = e0 if s4 < 2 else me0
                        return s4, buf, (s4 % 2) * 1024
                    for q4 in range(S // 512):
                        c0 = q4 * 512
                        par = (p * (S // 512) + q4) % 2
                        psA = ps[2 + par]

                        def emit_scores(kc, c0=c0, q4=q4):
                            sl = kc % 2
                            if staged:
                                stg = ([("sV", kgev[qb]) for qb in (2, 3)
                                        if qb < NB] if sl == 0 else [])
                            else:
                                stg = [("sV", qgev[qb])
                                       for qb in (2 * sl, 2 * sl + 1)
                                       if qb < NB]
                            wait("tensor", ("sV", kgev[kc // 4]),
                                 ("sV", qgev[q4]),
                                 ("sS", st_exp[sl]), *stg)
                            run("tensor", lambda h, kc=kc, kbuf=kbuf,
                                qa=qap(0, 64, c0, 512): h.matmul(
                                ps[sl][:, 0:512],
                                kbuf[0:64, kc * 128:kc * 128 + 128],
                                qa,
                                start=True, stop=True), None)
                            st_mms[sl] = run("tensor", lambda h, kc=kc,
                                             kbuf=kbuf,
                                             qa=qap(64, 128, c0, 512):
                                             h.matmul(
                                ps[sl][:, 512:1024],
                                kbuf[64:128, kc * 128:kc * 128 + 128],
                                qa,
                                start=True, stop=True), "sT")

                        def emit_mask(kc, c0=c0):
                            sl = kc % 2
                            wait("vector", ("sV", seg_ready),
                                 ("sV", st_mult[sl]))
                            st_mgen[sl] = run("vector", lambda h, kc=kc, sl=sl,
                                              c0=c0: h.tensor_scalar(
                                maskt[:, sl * 512:sl * 512 + 512],
                                segrep[:, c0:c0 + 512],
                                segidN[:, kc:kc + 1], None, OP.is_equal), "sV")

                        def emit_exp(kc):
                            sl = kc % 2
                            if masked:
                                wait("scalar", ("sT", st_mms[sl]),
                                     ("sT", st_av4[sl]),
                                     ("sV", st_mult[sl]))
                                dst = e0[:, sl * 1024:sl * 1024 + 1024]
                            else:
                                s4, buf, off = arena(kc)
                                wait("scalar", ("sT", st_mms[sl]),
                                     ("sT", st_av4[s4]))
                                dst = buf[:, off:off + 1024]
                            v = run("scalar", lambda h, sl=sl, dst=dst:
                                    h.activation(
                                dst, ps[sl][:, 0:1024], AF.Exp,
                                bias=f32(-28.0)), "sS")
                            st_exp[sl] = v
                            if not masked:
                                st_exp4[kc % 4] = v

                        def emit_mult(kc):
                            sl = kc % 2
                            wait("vector", ("sS", st_exp[sl]),
                                 ("sT", st_av[sl]), ("sT", st_av4[2 + sl]),
                                 ("sV", st_mgen[sl]))
                            mrep = bass.AP(maskt[:].tensor, sl * 512,
                                           [maskt[:].ap[0], [0, 2], [1, 512]])
                            st_mult[sl] = run("vector", lambda h, sl=sl,
                                              mrep=mrep: h.tensor_tensor(
                                me0[:, sl * 1024:sl * 1024 + 1024],
                                e0[:, sl * 1024:sl * 1024 + 1024],
                                mrep, OP.mult), "sV")

                        def emit_av(kc, psA=psA, par=par):
                            sl = kc % 2
                            if masked:
                                srcb = me0
                                soff = sl * 1024
                                wait("tensor", ("sV", st_mult[sl]))
                            else:
                                s4, srcb, soff = arena(kc)
                                wait("tensor", ("sS", st_exp4[s4]))
                            if kc == 0:
                                stg = [("sV", kgev[qb])
                                       for qb in (2 * par, 2 * par + 1)
                                       if qb < NB]
                                wait("tensor",
                                     ("sV", attn_state.get("avp%d" % par, 0)),
                                     attn_state.get("orp%d" % par),
                                     ("sS", attn_state.get("lnp%d" % par, 0)),
                                     ("sV", v_ready),
                                     *stg)
                            run("tensor", lambda h, kc=kc, soff=soff, srcb=srcb,
                                jr=jr, psA=psA: h.matmul(
                                psA[0:65, 0:512],
                                vg[:, kc * VW + jr * 161:kc * VW + jr * 161 + 65],
                                srcb[:, soff:soff + 512],
                                start=(kc == 0), stop=(kc == TC - 1)), None)
                            avm = run("tensor", lambda h, kc=kc, soff=soff,
                                      srcb=srcb, jr=jr, psA=psA: h.matmul(
                                psA[:, 512:1024],
                                vg[:, kc * VW + jr * 161 + 33:
                                   kc * VW + jr * 161 + 161],
                                srcb[:, soff + 512:soff + 1024],
                                start=(kc == 0), stop=(kc == TC - 1)), "sT")
                            st_av[sl] = avm
                            if not masked:
                                st_av4[kc % 4] = avm

                        emit_scores(0)
                        if masked:
                            emit_mask(0)
                        emit_exp(0)
                        if masked:
                            emit_mult(0)
                        if TC > 1:
                            emit_scores(1)
                            if masked:
                                emit_mask(1)
                        for kc in range(TC):
                            if kc + 1 < TC:
                                emit_exp(kc + 1)
                                if masked:
                                    emit_mult(kc + 1)
                            if kc + 2 < TC:
                                emit_scores(kc + 2)
                                if masked:
                                    emit_mask(kc + 2)
                            if kc == 0 and attn_state.get("tsc") is not None:
                                attn_state.pop("tsc")()
                            if kc == 1 and attn_state.get("pt") is not None:
                                attn_state.pop("pt")()
                            if (attn_state.get("nx_cq") is not None
                                    and kc in stage_kcs):
                                stage_q_part(q4, stage_kcs.index(kc))
                            if (attn_state.get("nx_kinj")
                                    and kc in stagek_kcs):
                                stage_k_part(q4, stagek_kcs.index(kc))
                            emit_av(kc)
                        av_last = cnt["sT"]

                        def tail_sc(psA=psA, par=par, av_last=av_last):
                            wait("scalar", ("sT", av_last),
                                 ("sT", attn_state.get("rep_done", 0)),
                                 ("sV", attn_state.get("avp%d" % par, 0)))
                            run("scalar", lambda h, psA=psA: h.activation(
                                xst[64:65, 1024:1536], psA[64:65, 0:512],
                                AF.Ln), "sS")
                            run("scalar", lambda h, psA=psA: h.activation(
                                xst[32:33, 1536:2048], psA[32:33, 512:1024],
                                AF.Ln), "sS")
                            wait("scalar", ("sS", cnt["sS"]))
                            run("scalar", lambda h: h.activation(
                                recb2[64:65, 0:512], xst[64:65, 1024:1536],
                                AF.Exp, scale=f32(-1.0)), "sS")
                            v_rec = run("scalar", lambda h: h.activation(
                                recb2[32:33, 512:1024], xst[32:33, 1536:2048],
                                AF.Exp, scale=f32(-1.0)), "sS")
                            attn_state["lnp%d" % par] = v_rec
                            attn_state["v_rec_last"] = v_rec
                        attn_state["tsc"] = tail_sc
                        wait("vector", ("sT", av_last),
                             ("sV", attn_state.get("avp%d" % par, 0)))
                        run("vector", lambda h, psA=psA, par=par: h.tensor_copy(
                            outraw[0:64, par * 512:par * 512 + 512],
                            psA[0:64, 0:512]), "sV")
                        s_or = run("vector", lambda h, psA=psA, par=par:
                                   h.tensor_copy(
                            outraw[64:128, par * 512:par * 512 + 512],
                            psA[64:128, 512:1024]), "sV")
                        attn_state["orp%d" % par] = ("sV", s_or)
                        attn_state["or_done"] = ("sV", s_or)

                        def tail_post(psA=psA, par=par, p=p, c0=c0, s_or=s_or,
                                      v_rec_=None):
                            wait("tensor", ("sV", s_or),
                                 ("sS", attn_state["v_rec_last"]))
                            run("tensor", lambda h, psA=psA: h.matmul(
                                psA[0:64, 0:512], onesf[64:65, 0:64],
                                recb2[64:65, 0:512], start=True, stop=True),
                                None)
                            mm_rep = run("tensor", lambda h, psA=psA: h.matmul(
                                psA[64:128, 512:1024], onesf[32:33, 0:64],
                                recb2[32:33, 512:1024], start=True, stop=True),
                                "sT")
                            attn_state["rep_done"] = mm_rep
                            wait("vector", ("sT", mm_rep))
                            run("vector", lambda h, p=p, c0=c0, psA=psA,
                                par=par: h.tensor_tensor(
                                Bb[0:64, p * S + c0:p * S + c0 + 512],
                                outraw[0:64, par * 512:par * 512 + 512],
                                psA[0:64, 0:512], OP.mult), "sV")
                            run("vector", lambda h, p=p, c0=c0, psA=psA,
                                par=par: h.tensor_tensor(
                                Bb[64:128, p * S + c0:p * S + c0 + 512],
                                outraw[64:128, par * 512:par * 512 + 512],
                                psA[64:128, 512:1024], OP.mult), "sV")
                            attn_state["norm_done"] = cnt["sV"]
                            attn_state["avp%d" % par] = cnt["sV"]
                        attn_state["pt"] = tail_post
            if attn_state.get("tsc") is not None:
                attn_state.pop("tsc")()
            if attn_state.get("pt") is not None:
                attn_state.pop("pt")()
            outT_ready_sV = attn_state["norm_done"]
            if debug and masked:
                sl_ = (TC - 1) % 2
                wait("gpsimd", ("sV", outT_ready_sV))
                run("gpsimd", lambda h, sl_=sl_: h.dma_start(
                    dbg["d_mask"][:, 0:512], maskt[:, sl_ * 512:sl_ * 512 + 512]),
                    "dG")
                run("gpsimd", lambda h, sl_=sl_: h.dma_start(
                    dbg["d_me"][:, 0:1024], me0[:, sl_ * 1024:sl_ * 1024 + 1024]),
                    "dG")
            if debug:
                nm_ = "d_ba" if not masked else "d_bs"
                wait("gpsimd", ("sV", outT_ready_sV))
                for m in range(HC):
                    run("gpsimd", lambda h, m=m, nm_=nm_: h.dma_start(
                        dbg[nm_][m * 128:(m + 1) * 128, :], til(Bb, m)), "dG")

            sel_last = 0
            if not masked:
                def wout_load(mb):
                    wslot = mb % 2
                    cw = 0
                    nh = min(4, HC)
                    for half in range(HC // nh):
                        cw = load_cols(wout, D, mb * 128, 128, 2 * wslot + half,
                                       wslot * 1024 + half * nh * 128,
                                       nchunk_lo=half * nh,
                                       nchunk_hi=(half + 1) * nh)
                    return cw
                cw = wout_load(0)
                for mb in range(DC):
                    wslot = mb % 2
                    nx = wout_load(mb + 1) if mb + 1 < DC else None
                    wait("tensor", cw, ("sV", outT_ready_sV),
                         ("sV", attn_state.get("wob_v", 0)),
                         attn_state.get("or_done"))
                    for c in range(HC):
                        run("tensor", lambda h, c=c, wslot=wslot, bv_col=bv_col:
                            h.matmul(
                            ps[2][:, 512:513],
                            wbf[:, wslot * 1024 + c * 128:wslot * 1024 + c * 128 + 128],
                            bvb[:, bv_col * 8 + c:bv_col * 8 + c + 1],
                            start=(c == 0), stop=(c == HC - 1)),
                            "sT" if c == HC - 1 else None)
                    mm_fold = cnt["sT"]
                    wait("vector", ("sT", mm_fold))
                    v_wob = run("vector", lambda h, mb=mb: h.tensor_scalar(
                        wob[:, wob_col * 8 + mb:wob_col * 8 + mb + 1],
                        ps[2][:, 512:513],
                        bias_sb[:, bo_col * 8 + mb:bo_col * 8 + mb + 1],
                        None, OP.add), "sV")
                    attn_state["wob_v"] = v_wob
                    for qc in range(NB):
                        wpar = (mb * NB + qc) % 2
                        pw = ps[wpar][:, 0:512]
                        wevp = attn_state.setdefault("wout_evp", {0: 0, 1: 0})
                        wait("tensor", ("sV", wevp[wpar]))
                        for c in range(HC):
                            run("tensor", lambda h, c=c, qc=qc, pw=pw, wslot=wslot:
                                h.matmul(
                                pw, wbf[:, wslot * 1024 + c * 128:
                                        wslot * 1024 + c * 128 + 128],
                                til(Bb, c, qc * 512, qc * 512 + 512),
                                start=(c == 0), stop=(c == HC - 1)),
                                "sT" if c == HC - 1 else None)
                        mmw = cnt["sT"]
                        mark_wbf(wslot * 1024, wslot * 1024 + 1024, mmw)
                        wait("vector", ("sT", mmw), ("sV", v_wob))
                        wevp[wpar] = run(
                            "vector", lambda h, mb=mb, qc=qc, pw=pw: h.tensor_scalar(
                                til(Cb, mb, qc * 512, qc * 512 + 512), pw,
                                wob[:, wob_col * 8 + mb:wob_col * 8 + mb + 1],
                                None, OP.add), "sV")
                    cw = nx
            else:
                def _ws(pool, need):
                    out = []
                    for t, w in pool:
                        for o in range(0, w - 511, 512):
                            out.append((t, o))
                            if len(out) == need:
                                return out
                    raise RuntimeError("no wout staging")
                dstw = _ws([(vg, TC * VW), (e0, 2048), (me0, 2048),
                            (maskt, 1024), (segrep, S)], DC * NB)

                def dstw_ap(mb, qc):
                    t, o = dstw[mb * NB + qc]
                    return t[:, o:o + 512]

                def wout_load2(mb):
                    wslot = mb % 2
                    cw = 0
                    nh = min(4, HC)
                    for half in range(HC // nh):
                        cw = load_cols(wout, D, mb * 128, 128, 2 * wslot + half,
                                       wslot * 1024 + half * nh * 128,
                                       nchunk_lo=half * nh,
                                       nchunk_hi=(half + 1) * nh)
                    return cw
                cw = wout_load2(0)
                for mb in range(DC):
                    wslot = mb % 2
                    nx = wout_load2(mb + 1) if mb + 1 < DC else None
                    wait("tensor", cw, ("sV", outT_ready_sV),
                         ("sV", attn_state.get("wob_v", 0)),
                         attn_state.get("or_done"))
                    if debug:
                        wait("vector", *all_lanes("dG"))
                    for c in range(HC):
                        run("tensor", lambda h, c=c, wslot=wslot,
                            bv_col=bv_col: h.matmul(
                            ps[2][:, 512:513],
                            wbf[:, wslot * 1024 + c * 128:
                                wslot * 1024 + c * 128 + 128],
                            bvb[:, bv_col * 8 + c:bv_col * 8 + c + 1],
                            start=(c == 0), stop=(c == HC - 1)),
                            "sT" if c == HC - 1 else None)
                    mm_fold = cnt["sT"]
                    wait("vector", ("sT", mm_fold))
                    attn_state["wob_v"] = run(
                        "vector", lambda h, mb=mb: h.tensor_scalar(
                            wob[:, wob_col * 8 + mb:wob_col * 8 + mb + 1],
                            ps[2][:, 512:513],
                            bias_sb[:, bo_col * 8 + mb:
                                    bo_col * 8 + mb + 1],
                            None, OP.add), "sV")
                    for qc in range(NB):
                        wpar = (mb * NB + qc) % 2
                        pw = ps[wpar][:, 0:512]
                        wevp = attn_state.setdefault("wout_evp", {0: 0, 1: 0})
                        wait("tensor", ("sV", wevp[wpar]))
                        for c in range(HC):
                            run("tensor", lambda h, c=c, qc=qc, pw=pw, wslot=wslot:
                                h.matmul(
                                pw, wbf[:, wslot * 1024 + c * 128:
                                        wslot * 1024 + c * 128 + 128],
                                til(Bb, c, qc * 512, qc * 512 + 512),
                                start=(c == 0), stop=(c == HC - 1)),
                                "sT" if c == HC - 1 else None)
                        mmw = cnt["sT"]
                        mark_wbf(wslot * 1024, wslot * 1024 + 1024, mmw)
                        wait("vector", ("sT", mmw),
                             ("sV", attn_state.get("wob_v", 0)))
                        wevp[wpar] = run(
                            "vector", lambda h, mb=mb, qc=qc, pw=pw:
                            h.tensor_scalar(
                                dstw_ap(mb, qc), pw,
                                wob[:, wob_col * 8 + mb:wob_col * 8 + mb + 1],
                                None, OP.add), "sV")
                    cw = nx
                wout_mm_all = cnt["sT"]
                wait("vector", ("sS", s_is1), ("sT", wout_mm_all))
                for mb in range(DC):
                    for qc in range(NB):
                        run("vector", lambda h, mb=mb, qc=qc: h.select(
                            til(Bb, mb, qc * 512, qc * 512 + 512),
                            is1B[:, qc * 512:qc * 512 + 512],
                            til(A, mb, qc * 512, qc * 512 + 512),
                            dstw_ap(mb, qc),
                            add_drain=True), "sV")
                    sel_last = cnt["sV"]
            return cnt["sV"]

        lt_done = mha(wqk_lt, wv_lt, bq=0, bk=1, bo_col=3, bv_col=0, wob_col=0,
                      wout=wout_lt, masked=False)
        if debug:
            wait("gpsimd", ("sV", lt_done))
            for m in range(DC):
                run("gpsimd", lambda h, m=m: h.dma_start(
                    dbg["d_long"][m * 128:(m + 1) * 128, :], til(Cb, m)), "dG", 16)
        st_done = mha(wqk_st, wv_st, bq=4, bk=5, bo_col=7, bv_col=1, wob_col=1,
                      wout=wout_st, masked=True)
        if debug:
            wait("gpsimd", ("sV", st_done))
            for m in range(DC):
                run("gpsimd", lambda h, m=m: h.dma_start(
                    dbg["d_sel"][m * 128:(m + 1) * 128, :], til(Bb, m)), "dG", 16)

        if debug:
            wait("scalar", *all_lanes("dG"))
            wait("vector", *all_lanes("dG"))
        xslot_user = [st_done, st_done]
        wait("sync", ("sT", cnt["sT"]), ("sV", cnt["sV"]))
        x1_tile_dG = [None] * DC
        x1slot_user = [(0, None), (0, None)]
        fus_cast = []
        fus_mm_last = 0
        def fus_load(mb):
            wslot = mb % 2
            cw = 0
            for quarter in range(4):
                cw = load_cols(fuswT, D, mb * 128, 128, quarter,
                               wslot * 2048 + quarter * 512, eng="scalar",
                               nchunk_lo=quarter * 4, nchunk_hi=quarter * 4 + 4)
            return cw
        cwf = fus_load(0)
        for mb in range(DC):
            wslot = mb % 2
            nxf = fus_load(mb + 1) if mb + 1 < DC else None
            cw = cwf
            wait("tensor", cw, ("sV", st_done), ("sV", lt_done))
            for qc in range(NB):
                j = (mb * NB + qc) % 2
                wait("sync", ("sV", xslot_user[j]))
                dx = run("sync", lambda h, mb=mb, qc=qc, j=j: h.dma_start(
                    xst[:, j * 512:j * 512 + 512],
                    xT[mb * 128:(mb + 1) * 128, qc * 512:qc * 512 + 512]), "dS")
                pw = ps[(mb * NB + qc) % 2][:, 0:512]
                wait("tensor", ("sV", xslot_user[j]))
                for c in range(2 * DC):
                    src = Cb if c < DC else Bb
                    run("tensor", lambda h, c=c, qc=qc, pw=pw, wslot=wslot, src=src:
                        h.matmul(
                        pw, wbf[:, wslot * 2048 + c * 128:wslot * 2048 + c * 128 + 128],
                        til(src, c % DC, qc * 512, qc * 512 + 512),
                        start=(c == 0), stop=(c == 2 * DC - 1)),
                        "sT" if c == 2 * DC - 1 else None)
                mmf = cnt["sT"]
                fus_mm_last = mmf
                mark_wbf(wslot * 2048, wslot * 2048 + 2048, mmf)
                wait("vector", ("sT", mmf), dx,
                     ("sS", x1slot_user[j][0]), x1slot_user[j][1])
                vx1 = run("vector", lambda h, mb=mb, j=j, pw=pw: h.scalar_tensor_tensor(
                    xst[:, (2 + j) * 512:(2 + j) * 512 + 512], pw,
                    bias_sb[:, 64 + mb:64 + mb + 1],
                    xst[:, j * 512:j * 512 + 512], OP.add, OP.add), "sV")
                xslot_user[j] = vx1
                wait("scalar", ("sV", vx1))
                sc = run("scalar", lambda h, mb=mb, qc=qc, j=j: h.activation(
                    til(A, mb, qc * 512, qc * 512 + 512),
                    xst[:, (2 + j) * 512:(2 + j) * 512 + 512], AF.Copy), "sS")
                fus_cast.append(sc)
                wait("gpsimd", ("sV", vx1))
                dg = run("gpsimd", lambda h, mb=mb, qc=qc, j=j: h.dma_start(
                    x1sc[mb * 128:(mb + 1) * 128, qc * 512:qc * 512 + 512],
                    xst[:, (2 + j) * 512:(2 + j) * 512 + 512]), "dG")
                x1slot_user[j] = (sc, dg)
                x1_tile_dG[mb] = dg
            cwf = nxf
        fus_cast_last = cnt["sS"]

        if debug:
            wait("gpsimd", *all_lanes("dG"))
            run("gpsimd", lambda h: h.dma_start(dbg["d_x1"][:, :], x1sc[:, :]), "dG")

        wait("tensor", ("sS", fus_cast_last))
        for qb in range(NB):
            for m in range(DC):
                run("tensor", lambda h, qb=qb, m=m: h.matmul(
                    psM(qb), ones_sc[:], til(A, m, qb * 512, qb * 512 + 512),
                    start=(m == 0), stop=(m == DC - 1)),
                    "sT" if m == DC - 1 else None)
        mm_mean2 = cnt["sT"]

        if debug:
            wait("vector", *all_lanes("dG"))
        xlo2 = []
        for m in range(DC):
            sl = m % 2
            if m >= 2:
                wait("gpsimd", ("sV", xlo2[m - 2]))
            wait("gpsimd", x1_tile_dG[m], ("sV", xlos[-1]),
                 ("sS", casts[-1]))
            d = run("gpsimd", lambda h, m=m, sl=sl: h.dma_start(
                xbig[:, sl * S:(sl + 1) * S], x1sc[m * 128:(m + 1) * 128, :]),
                "dG")
            wait("vector", d, ("sT", fus_mm_last))
            v = run("vector", lambda h, m=m, sl=sl: h.tensor_tensor(
                til(Bb, m), xbig[:, sl * S:(sl + 1) * S], til(A, m), OP.subtract),
                "sV")
            xlo2.append(v)

        wait("vector", ("sT", mm_mean2), ("sT", fus_mm_last))
        tmp2 = []
        for m in range(DC):
            wait("vector", ("sV", xlo2[m]))
            for qb in range(NB):
                run("vector", lambda h, m=m, qb=qb: h.tensor_tensor(
                    til(Cb, m, qb * 512, qb * 512 + 512),
                    til(Bb, m, qb * 512, qb * 512 + 512), psM(qb), OP.subtract), "sV")
            tmp2.append(cnt["sV"])
        c2_done = []
        for m in range(DC):
            wait("vector", ("sV", tmp2[m]))
            vc = run("vector", lambda h, m=m: h.tensor_tensor(
                til(Bb, m), til(A, m), til(Cb, m), OP.add), "sV")
            c2_done.append(vc)
        sq2 = []
        for m in range(DC):
            wait("scalar", ("sV", c2_done[m]))
            sc = run("scalar", lambda h, m=m: h.activation(
                til(Cb, m), til(Bb, m), AF.Square), "sS")
            sq2.append(sc)
        wait("tensor", ("sV", tmp2[-1]))
        for qb in range(NB):
            for m in range(DC):
                wait("tensor", ("sS", sq2[m]))
                run("tensor", lambda h, qb=qb, m=m: h.matmul(
                    psM(qb), ones_sc[:], til(Cb, m, qb * 512, qb * 512 + 512),
                    start=(m == 0), stop=(m == DC - 1)),
                    "sT" if m == DC - 1 else None)
        mm_var2 = cnt["sT"]
        wait("scalar", ("sT", mm_var2))
        for qb in range(NB):
            run("scalar", lambda h, qb=qb: h.activation(
                xbig[:, qb * 512:qb * 512 + 512], psM(qb), AF.Ln,
                bias=f32(1e-5)), "sS")
        s_ln2 = cnt["sS"]
        wait("scalar", ("sS", s_ln2))
        for qb in range(NB):
            run("scalar", lambda h, qb=qb: h.activation(
                xbig[:, S + qb * 512:S + qb * 512 + 512],
                xbig[:, qb * 512:qb * 512 + 512], AF.Exp,
                scale=f32(-0.5)), "sS")
        s_sq2 = cnt["sS"]
        wait("vector", ("sS", s_sq2))
        for m in range(DC):
            run("vector", lambda h, m=m: h.tensor_tensor(
                til(A, m), til(Bb, m), rstdB, OP.mult), "sV")
        ln2_ready = cnt["sV"]

        if debug:
            wait("gpsimd", ("sV", ln2_ready))
            for m in range(DC):
                run("gpsimd", lambda h, m=m: h.dma_start(
                    dbg["d_ln2"][m * 128:(m + 1) * 128, :], til(A, m)), "dG", 16)

        def ffn_stage(wd, src, dst, src_ready, dst_war):
            ev = 0

            def st_load(mb):
                wslot = mb % 2
                cw = 0
                for half in (0, 1):
                    cw = load_cols(wd, D, mb * 128, 128, 2 * wslot + half,
                                   wslot * 1024 + half * 512,
                                   nchunk_lo=half * 4, nchunk_hi=half * 4 + 4)
                return cw
            cw = st_load(0)
            for mb in range(DC):
                wslot = mb % 2
                nx = st_load(mb + 1) if mb + 1 < DC else None
                wait("tensor", cw, ("sV", src_ready), ("sV", dst_war))
                for qc in range(NB):
                    pw = ps[(mb * NB + qc) % 2][:, 0:512]
                    wait("tensor", ("sV", ev))
                    for c in range(DC):
                        run("tensor", lambda h, c=c, qc=qc, pw=pw, wslot=wslot:
                            h.matmul(
                            pw, wbf[:, wslot * 1024 + c * 128:
                                    wslot * 1024 + c * 128 + 128],
                            til(src, c, qc * 512, qc * 512 + 512),
                            start=(c == 0), stop=(c == DC - 1)),
                            "sT" if c == DC - 1 else None)
                    mmw = cnt["sT"]
                    mark_wbf(wslot * 1024, wslot * 1024 + 1024, mmw)
                    wait("vector", ("sT", mmw))
                    ev = run("vector", lambda h, mb=mb, qc=qc, pw=pw: h.tensor_copy(
                        til(dst, mb, qc * 512, qc * 512 + 512), pw), "sV")
                cw = nx
            return ev

        if debug:
            wait("vector", *all_lanes("dG"))

        sT_snap = cnt["sT"]
        sS_snap = cnt["sS"]
        def _slots(pool, need):
            out = []
            for t, w in pool:
                for o in range(0, w - 511, 512):
                    out.append((t, o))
                    if len(out) == need:
                        return out
            raise RuntimeError("not enough ffn staging slots")
        wt_slots = _slots([(e0, 2048), (me0, 2048), (qg, S), (kg, S),
                           (maskt, 1024), (segrep, S)], 2 * DC)
        a2_pool = [(vg, TC * VW), (outraw, 1024), (smallb, 3 * S + 64)]
        if DstX is not None:
            a2_pool.append((DstX, (DC - 2 * nDst) * 512))
        a2_slots = _slots(a2_pool, 2 * DC)
        assert not {(id(t), o) for t, o in wt_slots} & {(id(t), o) for t, o in a2_slots}

        def wt_ap(k, cc):
            t, o = wt_slots[2 * k + cc]
            return t[:, o:o + 512]

        def a2_ap(k, d2):
            t, o = a2_slots[2 * k + d2]
            return t[:, o:o + 512]

        rrslot = [0]

        def load_rows(dram, k, ap_fn):
            vs = []
            for hhf in range(2):
                sl2 = rrslot[0] % 4
                rrslot[0] += 1
                ws = wstate["wst"][sl2]
                wait("sync", ws if isinstance(ws, tuple) else None)
                d = run("sync", lambda h, k=k, hhf=hhf, sl2=sl2, dram=dram:
                        h.dma_start(
                    wst[:, sl2 * 512:sl2 * 512 + 512],
                    dram[k * 128:(k + 1) * 128, hhf * 512:hhf * 512 + 512]),
                    "dS")
                wait("scalar", d, ("sT", sT_snap),
                     ("sS", sS_snap), ("sV", st_done))
                v = run("scalar", lambda h, sl2=sl2, ap=ap_fn(k, hhf):
                        h.activation(
                    ap, wst[:, sl2 * 512:sl2 * 512 + 512], AF.Copy), "sS")
                wstate["wst"][sl2] = ("sS", v)
                vs.append(("sS", v))
            return vs[-1]

        wt_cast = [load_rows(w1d, k, wt_ap) for k in range(DC)]
        a2_cast = [load_rows(w3d, k, a2_ap) for k in range(DC)]

        def a1_load(a):
            wslot = a % 2
            cw = 0
            for half in (0, 1):
                cw = load_cols(w2d, D, a * 128, 128, 2 * wslot + half,
                               wslot * 1024 + half * 512, eng="scalar",
                               nchunk_lo=half * 4, nchunk_hi=half * 4 + 4)
            return cw
        ca1 = a1_load(0)
        ev = 0
        evp = {0: 0, 1: 0}
        for a in range(DC):
            wslot = a % 2
            nx1 = a1_load(a + 1) if a + 1 < DC else None
            wait("tensor", ca1, ("sV", ln2_ready), *wt_cast)
            for cc in range(2):
                par = (a * 2 + cc) % 2
                pw = ps[par][:, 0:512]
                wait("tensor", ("sV", evp[par]))
                for k in range(DC):
                    run("tensor", lambda h, a=a, cc=cc, k=k, pw=pw, wslot=wslot:
                        h.matmul(
                        pw, wbf[:, wslot * 1024 + k * 128:
                                wslot * 1024 + k * 128 + 128],
                        wt_ap(k, cc),
                        start=(k == 0), stop=(k == DC - 1)),
                        "sT" if k == DC - 1 else None)
                mmw = cnt["sT"]
                mark_wbf(wslot * 1024, wslot * 1024 + 1024, mmw)
                wait("vector", ("sT", mmw))
                ev = run("vector", lambda h, a=a, cc=cc, pw=pw: h.tensor_copy(
                    Bb[:, a * S + cc * 512:a * S + cc * 512 + 512], pw), "sV")
                evp[par] = ev
            ca1 = nx1
        m1_done = ev

        for cb in range(DC):
            for d2 in range(2):
                par = (cb * 2 + d2) % 2
                pw = ps[par][:, 0:512]
                wait("tensor", ("sV", evp[par]), ("sV", m1_done), *a2_cast)
                for a in range(DC):
                    run("tensor", lambda h, cb=cb, d2=d2, a=a, pw=pw: h.matmul(
                        pw, Bb[:, a * S + cb * 128:a * S + cb * 128 + 128],
                        a2_ap(a, d2),
                        start=(a == 0), stop=(a == DC - 1)),
                        "sT" if a == DC - 1 else None)
                mmw = cnt["sT"]
                wait("vector", ("sT", mmw))
                ev = run("vector", lambda h, cb=cb, d2=d2, pw=pw: h.tensor_copy(
                    wt_ap(cb, d2), pw), "sV")
                evp[par] = ev
        m2_done = ev

        NT3 = DC * NB
        NSL = min(8, (2 * S) // 512)
        s3pf = [None] * NT3
        s3_vo = [0] * NT3

        def s3_issue(t):
            mb, qc = divmod(t, NB)
            slot = t % NSL
            wait("gpsimd",
                 ("sV", s3_vo[t - NSL]) if t >= NSL else None,
                 x1_tile_dG[mb],
                 ("sV", ln2_ready) if t < NSL else None,
                 ("sS", s_sq2) if t < NSL else None)
            s3pf[t] = run("gpsimd", lambda h, mb=mb, qc=qc, slot=slot:
                          h.dma_start(
                xbig[:, slot * 512:slot * 512 + 512],
                x1sc[mb * 128:(mb + 1) * 128, qc * 512:qc * 512 + 512]), "dG")

        for t in range(min(NSL, NT3)):
            s3_issue(t)
        tslot_user = [0, 0, 0, 0]
        tanh_prev = [0, 0]
        oslot_user = [None, None, None, None]
        out_dG_last = 0
        for mb in range(DC):
            wait("tensor", ("sV", m2_done))
            for qc in range(NB):
                t = mb * NB + qc
                j = t % 2
                j4 = t % 4
                slot = t % NSL
                pw = ps[j][:, 0:512]
                wait("tensor", ("sS", tanh_prev[j]))
                for c in range(DC):
                    m2sl = wt_ap(c, mb // 4)
                    m2st = bass.AP(m2sl.tensor, m2sl.offset + (mb % 4) * 128,
                                   [m2sl.ap[0], [1, 128]])
                    run("tensor", lambda h, c=c, qc=qc, pw=pw, m2st=m2st:
                        h.matmul(
                        pw, m2st,
                        til(A, c, qc * 512, qc * 512 + 512),
                        start=(c == 0), stop=(c == DC - 1)),
                        "sT" if c == DC - 1 else None)
                mmw = cnt["sT"]
                wait("scalar", ("sT", mmw), ("sV", tslot_user[j4]))
                s_t = run("scalar", lambda h, mb=mb, j4=j4, pw=pw: h.activation(
                    a2_ap(j4 // 2, j4 % 2), pw, AF.Tanh,
                    bias=bias_sb[:, 72 + mb:72 + mb + 1]), "sS")
                tanh_prev[j] = s_t
                wait("vector", ("sS", s_t), s3pf[t], oslot_user[j4],
                     ("sV", xslot_user[j4]) if t < 4 and j4 < 2 else None,
                     ("sS", x1slot_user[j4 - 2][0])
                     if t < 4 and j4 >= 2 else None,
                     x1slot_user[j4 - 2][1] if t < 4 and j4 >= 2 else None)
                vo = run("vector", lambda h, j4=j4, slot=slot: h.tensor_tensor(
                    xst[:, j4 * 512:j4 * 512 + 512],
                    xbig[:, slot * 512:slot * 512 + 512],
                    a2_ap(j4 // 2, j4 % 2), OP.add), "sV")
                tslot_user[j4] = vo
                s3_vo[t] = vo
                wait("gpsimd", ("sV", vo))
                do = run("gpsimd", lambda h, mb=mb, qc=qc, j4=j4: h.dma_start(
                    outT[mb * 128:(mb + 1) * 128, qc * 512:qc * 512 + 512],
                    xst[:, j4 * 512:j4 * 512 + 512]), "dG")
                oslot_user[j4] = do
                out_dG_last = do
                if t + NSL < NT3:
                    s3_issue(t + NSL)

        wait("gpsimd", *all_lanes("dG"))
        wait("sync", *all_lanes("dS"))

        block = E(nc.Block())

        @block.sync
        def _(sync):
            for fn in Q["sync"]:
                fn(sync)

        @block.gpsimd
        def _(gpsimd):
            for fn in Q["gpsimd"]:
                fn(gpsimd)

        @block.scalar
        def _(scalar):
            for fn in Q["scalar"]:
                fn(scalar)

        @block.vector
        def _(vector):
            for fn in Q["vector"]:
                fn(vector)

        @block.tensor
        def _(tensor):
            for fn in Q["tensor"]:
                fn(tensor)

    return nc, dbg


def _prep_inputs(inputs, S=2048, H=16):
    NSEG = 8
    """Pure layout: slice/transpose/pack the reference inputs per core."""
    g = lambda k: np.ascontiguousarray(np.asarray(inputs[k], dtype=np.float32))
    x = g("L0_attn_x")
    B = x.shape[0]
    FFh = H * DH

    def qk_pack(w_in):
        wq = w_in[0:FFh]
        wk = w_in[FFh:2 * FFh]
        out = np.empty((D, 2 * FFh), np.float32)
        for p in range(H // 2):
            out[:, p * 256:p * 256 + 128] = wq[p * 128:(p + 1) * 128].T
            out[:, p * 256 + 128:p * 256 + 256] = wk[p * 128:(p + 1) * 128].T
        return np.ascontiguousarray(out)

    lt_in = g("L0_attn_lt_w_in")
    st_in = g("L0_attn_st_w_in")
    shared = {
        "wqk_lt": qk_pack(lt_in),
        "wqk_st": qk_pack(st_in),
        "wv_lt": np.ascontiguousarray(lt_in[2 * FFh:3 * FFh].T),
        "wv_st": np.ascontiguousarray(st_in[2 * FFh:3 * FFh].T),
        "wout_lt": np.ascontiguousarray(g("L0_attn_lt_w_out").T),
        "wout_st": np.ascontiguousarray(g("L0_attn_st_w_out").T),
        "fuswT": np.ascontiguousarray(g("L0_attn_fus_w").T),
        "w1": np.ascontiguousarray(g("L0_ffn_weight").T),
        "w2": g("L0_ffn_aux_weight1"),
        "w3": g("L0_ffn_aux_weight2"),
        "protosT": np.ascontiguousarray(
            g("L0_attn_protos").T.reshape(8, 128, NSEG).transpose(1, 0, 2)
            .reshape(128, 8 * NSEG)),
        "biases": np.ascontiguousarray(np.concatenate([
            v.reshape(-1, 128).T if v.size == D else
            np.pad(v.reshape(-1, 128).T, ((0, 0), (0, 8 - v.size // 128)))
            for v in [
                g("L0_attn_lt_b_in")[0:FFh], g("L0_attn_lt_b_in")[FFh:2 * FFh],
                g("L0_attn_lt_b_in")[2 * FFh:3 * FFh], g("L0_attn_lt_b_out"),
                g("L0_attn_st_b_in")[0:FFh], g("L0_attn_st_b_in")[FFh:2 * FFh],
                g("L0_attn_st_b_in")[2 * FFh:3 * FFh], g("L0_attn_st_b_out"),
                g("L0_attn_fus_b"), g("L0_ffn_bias")]], axis=1)),
    }
    in_maps = []
    for b in range(B):
        m = dict(shared)
        m["xT"] = np.ascontiguousarray(x[b].T)
        in_maps.append(m)
    return in_maps


def kernel(**inputs):
    S, H = 2048, 16
    in_maps = _prep_inputs(inputs, S, H)
    nc, _ = build_core_program(S=S, H=H, debug=False)
    res = run_bass_kernel_spmd(nc, in_maps, core_ids=list(range(len(in_maps))))
    out = np.stack([np.asarray(r["outT"]).T for r in res.results], axis=0)
    return np.ascontiguousarray(out.astype(np.float32))
